# version 03
# baseline (speedup 1.0000x reference)
import numpy as np
import ml_dtypes
from contextlib import ExitStack

import concourse.bass as bass
import concourse.mybir as mybir
from concourse.bass_utils import run_bass_kernel_spmd

F32 = mybir.dt.float32
BF16 = mybir.dt.bfloat16
AF = mybir.ActivationFunctionType
ALU = mybir.AluOpType
AX = mybir.AxisListType
BF = ml_dtypes.bfloat16

B = 2
SEQ = 2048
T = B * SEQ
D = 2048
NEG = -60000.0
BIG = 60000.0

DT = {"f32": F32, "bf16": BF16}
NPDT = {"f32": np.float32, "bf16": BF}


class Prog:
    ENGINES = ("sync", "tensor", "vector", "scalar", "gpsimd")

    def __init__(self, nc, es, name, sem_es=None):
        self.nc = nc
        self.es = es
        self.sem_es = sem_es if sem_es is not None else es
        self.name = name
        self.steps = {e: [] for e in self.ENGINES}
        self.sem = {}
        self.cnt = {}

    def _s(self, key):
        if key not in self.sem:
            self.sem[key] = self.sem_es.enter_context(
                self.nc.semaphore(f"{self.name}_{key}"))
            self.cnt[key] = 0
        return self.sem[key]

    def barrier(self):
        waits = tuple((k, self.cnt[k]) for k in self.cnt if self.cnt[k] > 0)
        if not waits:
            return
        for ename in self.ENGINES:
            self.step(ename, lambda e: None, wait=waits)

    def tok(self, key):
        self._s(key)
        return self.cnt[key]

    def step(self, eng, fn, wait=(), inc=()):
        waits = [(self._s(k), v) for (k, v) in wait]
        incs = []
        for k, amt in inc:
            self._s(k)
            self.cnt[k] += amt
            incs.append((self.sem[k], amt))
        self.steps[eng].append((waits, fn, incs))
        return {k: self.cnt[k] for (k, _) in inc}

    def dma(self, eng, out, in_, key="ld", wait=()):
        def fn(e):
            return e.dma_start(out=out, in_=in_)
        self.step(eng, fn, wait=wait, inc=((key, 16),))
        return self.cnt[key]

    def comp(self, eng, fn, key=None, wait=()):
        inc = ((key, 1),) if key else ()
        self.step(eng, fn, wait=wait, inc=inc)
        return self.cnt[key] if key else None

    def finalize(self):
        waits = tuple((k, self.cnt[k]) for k in ("ld", "st")
                      if k in self.cnt and self.cnt[k] > 0)
        if not waits:
            return
        for ename in self.ENGINES:
            if self.steps[ename]:
                self.step(ename, lambda e: None, wait=waits)
                return

    OWN = {"vector": "vd", "scalar": "sd", "gpsimd": "pd", "tensor": "td"}

    def emit(self, block):
        for ename in self.ENGINES:
            steps = self.steps[ename]
            if not steps:
                continue
            own = self.OWN.get(ename)

            def run(eng, steps=steps, own=own):
                last_own = 0
                for waits, fn, incs in steps:
                    if own is not None and last_own > 0:
                        eng.wait_ge(self.sem[own], last_own)
                    for sem, val in waits:
                        eng.wait_ge(sem, val)
                    inst = fn(eng)
                    for sem, amt in incs:
                        assert inst is not None, "inc on step returning None"
                        inst = inst.then_inc(sem, amt)
                        if own is not None and sem is self.sem.get(own):
                            last_own += amt
            getattr(block, ename)(run)


def rearr_fp(dram_ap, p=128):
    return dram_ap.rearrange("(fc p) t -> p fc t", p=p)


def host_consts():
    c = {}
    c["IDF32"] = np.eye(128, dtype=np.float32)
    c["IDBF"] = np.eye(128, dtype=BF)
    c["ONES128"] = np.ones((128, 1), np.float32)
    c["ONES1"] = np.ones((1, 128), np.float32)
    cm = np.zeros((4, 128, 512), np.float32)
    for r in range(4):
        q = np.arange(128)[:, None]
        k = np.arange(512)[None, :]
        cm[r] = np.where(k <= r * 128 + q, 0.0, NEG)
    c["CMASK"] = cm
    tau = np.arange(128)[:, None]
    j = np.arange(128)[None, :]
    c["MASKS_POS"] = np.where(j >= tau, BIG, 0.0).astype(np.float32)
    c["MASKU_INC"] = np.where(j >= tau, 0.0, NEG).astype(np.float32)
    eh = np.zeros((16, 16, 128), np.float32)
    for h in range(16):
        eh[h, h, :] = 1.0
    c["EH"] = eh
    return c


CONST_SPECS = {
    "IDF32": ([128, 128], "f32"),
    "IDBF": ([128, 128], "bf16"),
    "ONES128": ([128, 1], "f32"),
    "ONES1": ([1, 128], "f32"),
    "CMASK": ([4, 128, 512], "f32"),
    "MASKS_POS": ([128, 128], "f32"),
    "MASKU_INC": ([128, 128], "f32"),
    "EH": ([16, 16, 128], "f32"),
}


PREP_CASTS = [
    ("L0_attn_Wq", "WQB", [2048, 2048]),
    ("L0_attn_Wk", "WKB", [2048, 512]),
    ("L0_attn_Wv", "WVB", [2048, 512]),
    ("L0_attn_Wo", "WOB", [2048, 2048]),
]
PREP_TRANS = [
    ("L1_attn_w_qkvz", "WQKVZT", 6144, 2048),
    ("L1_attn_w_ba", "WBAT", 32, 2048),
    ("L1_attn_w_out", "WOUTT", 2048, 2048),
    ("L1_ffn_sw1", "SW1T", 1024, 2048),
    ("L1_ffn_swg", "SWGT", 1024, 2048),
    ("L1_ffn_sw2", "SW2T", 2048, 1024),
    ("L1_ffn_Wdown", "WDOWNT", 512, 2048),
    ("L1_ffn_Wg", "WGT", 16, 2048),
    ("L1_ffn_Wup", "WUPT", 2048, 512),
]


def emit_prep_w(p, io):
    nc = p.nc
    es = p.es
    for src, dst, _shape in PREP_CASTS:
        p.dma("gpsimd", io[dst][:], io[src][:], key="st")

    src_sb = [es.enter_context(nc.sbuf_tensor(f"pw_src{i}", [128, 2048], BF16))
              for i in range(2)]
    stg_sb = [es.enter_context(nc.sbuf_tensor(f"pw_stg{i}", [128, 2048 * 2], BF16))
              for i in range(2)]
    ps = es.enter_context(nc.psum_tensor("pw_ps", [128, 1024], BF16))
    idb = es.enter_context(nc.sbuf_tensor("pw_idb", [128, 128], BF16))
    p.dma("gpsimd", idb[:], io["IDBF"][:])
    ld0 = p.tok("ld")

    jobs = []
    for src, dst, M, K in PREP_TRANS:
        jobs.append((io[src], io[dst], M, K))
    for e in range(16):
        jobs.append((io["L1_ffn_rw1"][e], io["RW1T"][e], 1024, 512))
        jobs.append((io["L1_ffn_rwg"][e], io["RWGT"][e], 1024, 512))
        jobs.append((io["L1_ffn_rw2"][e], io["RW2T"][e], 512, 1024))

    it = 0
    for src, dst, M, K in jobs:
        nk = K // 128
        dstv = dst.rearrange("(kc q) m -> q kc m", q=128)
        for mb in range((M + 127) // 128):
            rows = min(128, M - mb * 128)
            buf = it % 2
            p.dma("gpsimd", src_sb[buf][0:rows, 0:K],
                  src[mb * 128: mb * 128 + rows, :],
                  key="ld", wait=(("td", p.tok("td")),))
            ldt = p.tok("ld")
            for g0 in range(0, nk, 8):
                gn = min(8, nk - g0)

                def tfn(t, buf=buf, g0=g0, gn=gn, rows=rows):
                    for i in range(gn):
                        inst = t.transpose(
                            ps[0:128, i * 128: i * 128 + rows],
                            src_sb[buf][0:rows, (g0 + i) * 128:(g0 + i + 1) * 128],
                            idb[0:rows, 0:rows])
                    return inst
                p.comp("tensor", tfn, key="td",
                       wait=(("ld", ldt), ("vd", p.tok("vd"))))
                tdt = p.tok("td")

                def vfn(v, buf=buf, g0=g0, gn=gn, rows=rows):
                    if rows != 128:
                        return v.tensor_copy(
                            stg_sb[buf][0:128, g0 * 128:(g0 + gn) * 128]
                            .rearrange("q (kc m) -> q kc m", m=128)[:, :, 0:rows],
                            ps[0:128, 0:gn * 128]
                            .rearrange("q (kc m) -> q kc m", m=128)[:, :, 0:rows])
                    return v.tensor_copy(
                        stg_sb[buf][0:128, g0 * 128:(g0 + gn) * 128],
                        ps[0:128, 0:gn * 128])
                p.comp("vector", vfn, key="vd",
                       wait=(("td", tdt), ("st", p.tok("st"))))
            p.dma("scalar",
                  dstv[:, :, mb * 128: mb * 128 + rows],
                  stg_sb[buf][0:128, 0:nk * 128]
                  .rearrange("q (kc m) -> q kc m", m=128)[:, :, 0:rows],
                  key="st", wait=(("vd", p.tok("vd")),))
            it += 1


def emit_bitq(p, io):
    nc, es = p.nc, p.es
    w_specs = [
        ("L0_ffn_w1", "Q1T", 5632, 2048),
        ("L0_ffn_w2", "Q2T", 5632, 2048),
        ("L0_ffn_w3", "Q3T", 2048, 5632),
    ]
    src32 = [es.enter_context(nc.sbuf_tensor(f"bq_s{i}", [128, 5632], F32))
             for i in range(2)]
    acc = es.enter_context(nc.sbuf_tensor("bq_acc", [128, 8], F32))
    part = es.enter_context(nc.sbuf_tensor("bq_part", [128, 8], F32))
    ones = es.enter_context(nc.sbuf_tensor("bq_ones", [128, 1], F32))
    ones1 = es.enter_context(nc.sbuf_tensor("bq_ones1", [1, 128], F32))
    throw = es.enter_context(nc.sbuf_tensor("bq_throw", [1, 4], F32))
    thcol = es.enter_context(nc.sbuf_tensor("bq_thcol", [128, 8], F32))
    pstat = es.enter_context(nc.psum_tensor("bq_pstat", [128, 8], F32))
    qt = [es.enter_context(nc.sbuf_tensor(f"bq_q{i}", [128, 5632], BF16))
          for i in range(2)]
    stg = [es.enter_context(nc.sbuf_tensor(f"bq_stg{i}", [128, 5632], BF16))
           for i in range(2)]
    psq = es.enter_context(nc.psum_tensor("bq_psq", [128, 1024], BF16))
    idb = es.enter_context(nc.sbuf_tensor("bq_idb", [128, 128], BF16))
    p.dma("gpsimd", idb[:], io["IDBF"][:])
    p.dma("sync", ones[:], io["ONES128"][:])
    p.dma("sync", ones1[:], io["ONES1"][:])

    it = 0
    for wi, (src_name, _, M, K) in enumerate(w_specs):
        src = io[src_name]
        nmb = M // 128
        for mb in range(nmb):
            buf = it % 2
            p.dma("sync", src32[buf][:, 0:K], src[mb * 128:(mb + 1) * 128, :],
                  key="ld", wait=(("vd", p.tok("vd")),))
            ldt = p.tok("ld")

            def vred(v, buf=buf, wi=wi, K=K):
                return v.reduce_sum(part[:, wi:wi + 1], src32[buf][:, 0:K],
                                    axis=AX.X, apply_absolute_value=True)
            p.comp("vector", vred, key="vd", wait=(("ld", ldt),))

            def vacc(v, wi=wi, mb=mb):
                if mb == 0:
                    return v.tensor_copy(acc[:, wi:wi + 1], part[:, wi:wi + 1])
                return v.tensor_add(acc[:, wi:wi + 1], acc[:, wi:wi + 1],
                                    part[:, wi:wi + 1])
            p.comp("vector", vacc, key="vd")
            it += 1
    vd_all = p.tok("vd")

    def tstat(t):
        return t.matmul(pstat[0:1, 0:4], ones[:], acc[:, 0:4],
                        start=True, stop=True)
    p.comp("tensor", tstat, key="td", wait=(("vd", vd_all),))

    def vth(v):
        n = 5632.0 * 2048.0
        return v.tensor_scalar(throw[0:1, 0:4], pstat[0:1, 0:4],
                               0.5 / n, 0.5e-8, ALU.mult, ALU.add)
    p.comp("vector", vth, key="vd", wait=(("td", p.tok("td")),))

    def tbc(t):
        return t.matmul(pstat[0:128, 4:8], ones1[:], throw[0:1, 0:4],
                        start=True, stop=True)
    p.comp("tensor", tbc, key="td", wait=(("vd", p.tok("vd")),))

    def vbc(v):
        v.tensor_copy(thcol[:, 0:4], pstat[:, 4:8])
        return v.tensor_scalar_mul(thcol[:, 4:8], pstat[:, 4:8], -1.0)
    p.comp("vector", vbc, key="vd", wait=(("td", p.tok("td")),))
    prep_vd = p.tok("vd")

    it = 0
    for wi, (src_name, dst_name, M, K) in enumerate(w_specs):
        src, dst = io[src_name], io[dst_name]
        dstv = dst.rearrange("(kc q) m -> q kc m", q=128)
        nk = K // 128
        for mb in range(M // 128):
            buf = it % 2
            p.dma("sync", src32[buf][:, 0:K], src[mb * 128:(mb + 1) * 128, :],
                  key="ld", wait=(("vd", max(prep_vd, p.tok("vd"))),))
            ldt = p.tok("ld")

            def vq(v, buf=buf, wi=wi, K=K):
                v.tensor_single_scalar(qt[buf][:, 0:K], src32[buf][:, 0:K],
                                       thcol[:, wi:wi + 1], ALU.is_ge)
                v.tensor_single_scalar(src32[buf][:, 0:K], src32[buf][:, 0:K],
                                       thcol[:, 4 + wi:5 + wi], ALU.is_le)
                return v.tensor_sub(qt[buf][:, 0:K], qt[buf][:, 0:K],
                                    src32[buf][:, 0:K])
            p.comp("vector", vq, key="vd", wait=(("ld", ldt),))
            qvd = p.tok("vd")
            for g0 in range(0, nk, 8):
                gn = min(8, nk - g0)

                def tfn(t, buf=buf, g0=g0, gn=gn):
                    for i in range(gn):
                        inst = t.transpose(
                            psq[:, i * 128:(i + 1) * 128],
                            qt[buf][:, (g0 + i) * 128:(g0 + i + 1) * 128],
                            idb[:])
                    return inst
                p.comp("tensor", tfn, key="td",
                       wait=(("vd", qvd if g0 == 0 else p.tok("vd")),))

                def vev(v, buf=buf, g0=g0, gn=gn):
                    return v.tensor_copy(stg[buf][:, g0 * 128:(g0 + gn) * 128],
                                         psq[:, 0:gn * 128])
                p.comp("vector", vev, key="vd", wait=(("td", p.tok("td")),))
            p.dma("scalar", dstv[:, :, mb * 128:(mb + 1) * 128],
                  stg[buf][:, 0:nk * 128].rearrange("q (kc m) -> q kc m", m=128),
                  key="st", wait=(("vd", p.tok("vd")),))
            it += 1


def emit_xin(p, io):
    nc, es = p.nc, p.es
    x0 = io["X0"]
    xf = io["XF0"]
    xfv = rearr_fp(xf)
    src = [es.enter_context(nc.sbuf_tensor(f"xi_s{i}", [128, 2048], F32))
           for i in range(2)]
    stg = [es.enter_context(nc.sbuf_tensor(f"xi_g{i}", [128, 2048], F32))
           for i in range(2)]
    ps = es.enter_context(nc.psum_tensor("xi_ps", [128, 2048], F32))
    idf = es.enter_context(nc.sbuf_tensor("xi_id", [128, 128], F32))
    p.dma("sync", idf[:], io["IDF32"][:])

    it = 0
    for s in range(B):
        for ti in range(SEQ // 128):
            buf = it % 2
            p.dma("sync", src[buf][:],
                  x0[s, ti * 128:(ti + 1) * 128, :], key="ld",
                  wait=(("td", p.tok("td")),))
            ldt = p.tok("ld")
            evd = p.tok("vd")
            for g in range(4):
                def tfn(t, buf=buf, g=g):
                    for i in range(4):
                        fc = g * 4 + i
                        inst = t.transpose(ps[:, g * 512 + i * 128: g * 512 + (i + 1) * 128],
                                           src[buf][:, fc * 128:(fc + 1) * 128],
                                           idf[:])
                    return inst
                p.comp("tensor", tfn, key="td",
                       wait=(("ld", ldt), ("vd", evd)))

                def vfn(v, buf=buf, g=g):
                    return v.tensor_copy(stg[buf][:, g * 512:(g + 1) * 512],
                                         ps[:, g * 512:(g + 1) * 512])
                p.comp("vector", vfn, key="vd", wait=(("td", p.tok("td")),))
            tcol = s * SEQ + ti * 128
            p.dma("scalar", xfv[:, :, tcol:tcol + 128],
                  stg[buf][:].rearrange("q (fc t) -> q fc t", t=128),
                  key="st", wait=(("vd", p.tok("vd")),))
            it += 1


def emit_ln(p, io, src_name, dst_name, tag="ln"):
    nc, es = p.nc, p.es
    xfv = rearr_fp(io[src_name])
    xnv = rearr_fp(io[dst_name])
    xblk = [es.enter_context(nc.sbuf_tensor(f"{tag}_x{i}", [128, 16 * 512], F32))
            for i in range(2)]
    sq = es.enter_context(nc.sbuf_tensor(f"{tag}_sq", [128, 16 * 512], F32))
    xnb = [es.enter_context(nc.sbuf_tensor(f"{tag}_o{i}", [128, 16 * 512], BF16))
           for i in range(2)]
    tmp = es.enter_context(nc.sbuf_tensor(f"{tag}_tmp", [128, 512], F32))
    rowbuf = es.enter_context(nc.sbuf_tensor(f"{tag}_row", [1, 512 * 4], F32))
    ones = es.enter_context(nc.sbuf_tensor(f"{tag}_ones", [128, 1], F32))
    ones1 = es.enter_context(nc.sbuf_tensor(f"{tag}_ones1", [1, 128], F32))
    ps_s = es.enter_context(nc.psum_tensor(f"{tag}_ps", [1, 1024], F32))
    ps_bc = es.enter_context(nc.psum_tensor(f"{tag}_bc", [128, 1024], F32))
    p.dma("sync", ones[:], io["ONES128"][:])
    p.dma("sync", ones1[:], io["ONES1"][:])

    MU, RS = 0, 1
    for tbi in range(T // 512):
        buf = tbi % 2
        tsl = slice(tbi * 512, (tbi + 1) * 512)
        p.dma("sync", xblk[buf][:].rearrange("q (fc t) -> q fc t", t=512),
              xfv[:, :, tsl], key="ld",
              wait=(("vd", p.tok("vd")), ("td", p.tok("td"))))
        ldt = p.tok("ld")

        def sfn(s, buf=buf):
            return s.activation(sq[:], xblk[buf][:], AF.Square)
        p.comp("scalar", sfn, key="sd", wait=(("ld", ldt), ("td", p.tok("td"))))
        sdt = p.tok("sd")

        def tfn(t, buf=buf):
            for fc in range(16):
                t.matmul(ps_s[0:1, 0:512], ones[:],
                         xblk[buf][:, fc * 512:(fc + 1) * 512],
                         start=(fc == 0), stop=(fc == 15))
            for fc in range(16):
                inst = t.matmul(ps_s[0:1, 512:1024], ones[:],
                                sq[:, fc * 512:(fc + 1) * 512],
                                start=(fc == 0), stop=(fc == 15))
            return inst
        p.comp("tensor", tfn, key="td",
               wait=(("ld", ldt), ("sd", sdt), ("vd", p.tok("vd"))))
        tdt = p.tok("td")

        def vstat(v):
            v.tensor_scalar_mul(rowbuf[0:1, 0:512], ps_s[0:1, 0:512], 1.0 / D)
            v.tensor_scalar_mul(rowbuf[0:1, 512:1024], ps_s[0:1, 512:1024], 1.0 / D)
            v.tensor_mul(rowbuf[0:1, 1024:1536], rowbuf[0:1, 0:512], rowbuf[0:1, 0:512])
            v.tensor_sub(rowbuf[0:1, 512:1024], rowbuf[0:1, 512:1024],
                         rowbuf[0:1, 1024:1536])
            return v.tensor_scalar_add(rowbuf[0:1, 512:1024],
                                       rowbuf[0:1, 512:1024], 1e-5)
        p.comp("vector", vstat, key="vd", wait=(("td", tdt),))

        def s_sqrt(s):
            return s.activation(rowbuf[0:1, 1024:1536], rowbuf[0:1, 512:1024],
                                AF.Sqrt)
        p.comp("scalar", s_sqrt, key="sd", wait=(("vd", p.tok("vd")),))

        def v_rec(v):
            return v.reciprocal(rowbuf[0:1, 1536:2048], rowbuf[0:1, 1024:1536])
        p.comp("vector", v_rec, key="vd", wait=(("sd", p.tok("sd")),))

        def t_bc(t):
            t.matmul(ps_bc[:, 0:512], ones1[:], rowbuf[0:1, 0:512],
                     start=True, stop=True)
            return t.matmul(ps_bc[:, 512:1024], ones1[:], rowbuf[0:1, 1536:2048],
                            start=True, stop=True)
        p.comp("tensor", t_bc, key="td", wait=(("vd", p.tok("vd")),))
        bct = p.tok("td")

        def v_apply(v, buf=buf):
            for fc in range(16):
                v.tensor_sub(tmp[:], xblk[buf][:, fc * 512:(fc + 1) * 512],
                             ps_bc[:, 0:512])
                inst = v.tensor_mul(xnb[buf][:, fc * 512:(fc + 1) * 512],
                                    tmp[:], ps_bc[:, 512:1024])
            return inst
        p.comp("vector", v_apply, key="vd",
               wait=(("td", bct), ("st", p.tok("st"))))
        p.dma("scalar", xnv[:, :, tsl],
              xnb[buf][:].rearrange("q (fc t) -> q fc t", t=512),
              key="st", wait=(("vd", p.tok("vd")),))


def emit_sweep(p, io, specs, tb, tag):
    nc, es = p.nc, p.es
    xsrcs = {}
    for sp in specs:
        for lists in (sp["klist"], sp.get("klist2") or []):
            for wt, xn, K in lists:
                xsrcs[xn] = K // 128
    xblk = {xn: es.enter_context(
        nc.sbuf_tensor(f"{tag}_x_{xn}", [128, nkc * tb], BF16))
        for xn, nkc in xsrcs.items()}
    maxslots = max(sum(K // 128 for _, _, K in sp["klist"]) +
                   sum(K // 128 for _, _, K in (sp.get("klist2") or []))
                   for sp in specs)
    wtile = [es.enter_context(
        nc.sbuf_tensor(f"{tag}_w{i}", [128, maxslots * 128], BF16))
        for i in range(2)]
    stage = [es.enter_context(nc.sbuf_tensor(f"{tag}_o{i}", [128, tb], F32))
             for i in range(2)]
    stageb = [es.enter_context(nc.sbuf_tensor(f"{tag}_ob{i}", [128, tb], BF16))
              for i in range(2)]
    aux_sb = [es.enter_context(nc.sbuf_tensor(f"{tag}_a{i}", [128, tb], F32))
              for i in range(2)]
    tmp1 = es.enter_context(nc.sbuf_tensor(f"{tag}_t1", [128, tb], F32))
    ps = [es.enter_context(nc.psum_tensor(f"{tag}_p{i}", [128, tb], F32))
          for i in range(2)]
    paired = any(sp.get("klist2") for sp in specs)
    ps2 = [es.enter_context(nc.psum_tensor(f"{tag}_q{i}", [128, tb], F32))
           for i in range(2)] if paired else None

    it = 0
    wtok = [0, 0]
    atok = [0, 0]
    sdtok = [0, 0]
    stok = [0, 0]
    for tbi in range(T // tb):
        tsl = slice(tbi * tb, (tbi + 1) * tb)
        for xn, nkc in xsrcs.items():
            p.dma("sync", xblk[xn][:].rearrange("q (kc t) -> q kc t", t=tb),
                  rearr_fp(io[xn])[:, :, tsl], key="ld",
                  wait=(("td", p.tok("td")),))
        xld = p.tok("ld")
        for sp in specs:
            M = sp["M"]
            epi = sp["epi"]
            for mb in range((M + 127) // 128):
                rows = min(128, M - mb * 128)
                buf = it % 2
                slot = 0
                slots1, slots2 = [], []
                for gi, (wt, xn, K) in enumerate(sp["klist"]):
                    nkc = K // 128
                    p.dma("sync",
                          wtile[buf][0:128, slot * 128:(slot + nkc) * 128]
                          .rearrange("q (kc m) -> q kc m", m=128)[:, :, 0:rows],
                          io[wt].rearrange("(kc q) m -> q kc m", q=128)
                          [:, :, mb * 128: mb * 128 + rows],
                          key="ld", wait=(("td", wtok[buf]),))
                    slots1.append((slot, xn, nkc))
                    slot += nkc
                for gi, (wt, xn, K) in enumerate(sp.get("klist2") or []):
                    nkc = K // 128
                    p.dma("sync",
                          wtile[buf][0:128, slot * 128:(slot + nkc) * 128]
                          .rearrange("q (kc m) -> q kc m", m=128)[:, :, 0:rows],
                          io[wt].rearrange("(kc q) m -> q kc m", q=128)
                          [:, :, mb * 128: mb * 128 + rows],
                          key="ld", wait=(("td", wtok[buf]),))
                    slots2.append((slot, xn, nkc))
                    slot += nkc
                wld = p.tok("ld")
                if epi == "resid":
                    p.dma("sync", aux_sb[buf][0:rows, :],
                          io[sp["aux"]][mb * 128:mb * 128 + rows, tsl],
                          key="ld", wait=(("vd", atok[buf]),))
                auxld = p.tok("ld")

                def mmfn(t, buf=buf, rows=rows, slots1=slots1, slots2=slots2):
                    for slots, pp in ((slots1, ps), (slots2, ps2)):
                        if not slots:
                            continue
                        nktot = sum(nkc for _, _, nkc in slots)
                        ki = 0
                        for slot0, xn, nkc in slots:
                            for kc in range(nkc):
                                for nb in range(tb // 512):
                                    inst = t.matmul(
                                        pp[buf][0:rows, nb * 512:(nb + 1) * 512],
                                        wtile[buf][:, (slot0 + kc) * 128:
                                                   (slot0 + kc) * 128 + rows],
                                        xblk[xn][:, kc * tb + nb * 512:
                                                 kc * tb + (nb + 1) * 512],
                                        start=(ki == 0), stop=(ki == nktot - 1))
                                ki += 1
                    return inst
                p.comp("tensor", mmfn, key="td",
                       wait=(("ld", auxld if epi == "resid" else wld),
                             ("vd", atok[buf]), ("sd", sdtok[buf])))
                wtok[buf] = p.tok("td")
                mmt = wtok[buf]

                out = io[sp["out"]] if sp["out"] else None
                if epi == "bf16" or epi.startswith("scale:"):
                    fac = float(epi.split(":")[1]) if ":" in epi else 1.0

                    def vep(v, buf=buf, rows=rows, fac=fac):
                        if fac != 1.0:
                            return v.tensor_scalar_mul(stageb[buf][0:rows, :],
                                                       ps[buf][0:rows, :], fac)
                        return v.tensor_copy(stageb[buf][0:rows, :],
                                             ps[buf][0:rows, :])
                    p.comp("vector", vep, key="vd",
                           wait=(("td", mmt), ("st", stok[buf])))
                    atok[buf] = p.tok("vd")
                    p.dma("scalar", out[mb * 128:mb * 128 + rows, tsl],
                          stageb[buf][0:rows, :], key="st",
                          wait=(("vd", p.tok("vd")),))
                elif epi == "f32":
                    def vep(v, buf=buf, rows=rows):
                        return v.tensor_copy(stage[buf][0:rows, :],
                                             ps[buf][0:rows, :])
                    p.comp("vector", vep, key="vd",
                           wait=(("td", mmt), ("st", stok[buf])))
                    atok[buf] = p.tok("vd")
                    p.dma("scalar", out[mb * 128:mb * 128 + rows, tsl],
                          stage[buf][0:rows, :], key="st",
                          wait=(("vd", p.tok("vd")),))
                elif epi == "resid":
                    def vep(v, buf=buf, rows=rows):
                        return v.tensor_add(stage[buf][0:rows, :],
                                            ps[buf][0:rows, :],
                                            aux_sb[buf][0:rows, :])
                    p.comp("vector", vep, key="vd",
                           wait=(("td", mmt), ("st", stok[buf])))
                    atok[buf] = p.tok("vd")
                    p.dma("scalar", out[mb * 128:mb * 128 + rows, tsl],
                          stage[buf][0:rows, :], key="st",
                          wait=(("vd", p.tok("vd")),))
                elif epi in ("bitglu", "sqreluglu"):
                    if epi == "bitglu":
                        def sep(s, buf=buf, rows=rows):
                            return s.activation(stage[buf][0:rows, :],
                                                ps[buf][0:rows, :], AF.Sigmoid)
                        p.comp("scalar", sep, key="sd",
                               wait=(("td", mmt), ("vd", atok[buf])))
                        sdtok[buf] = p.tok("sd")

                        def vep(v, buf=buf, rows=rows):
                            return v.tensor_mul(stageb[buf][0:rows, :],
                                                stage[buf][0:rows, :],
                                                ps2[buf][0:rows, :])
                        p.comp("vector", vep, key="vd",
                               wait=(("sd", p.tok("sd")), ("st", stok[buf])))
                    else:
                        def sep(s, buf=buf, rows=rows):
                            return s.activation(stage[buf][0:rows, :],
                                                ps2[buf][0:rows, :], AF.Relu)
                        p.comp("scalar", sep, key="sd",
                               wait=(("td", mmt), ("vd", atok[buf])))
                        sdtok[buf] = p.tok("sd")

                        def vep(v, buf=buf, rows=rows):
                            v.tensor_mul(tmp1[0:rows, :], stage[buf][0:rows, :],
                                         stage[buf][0:rows, :])
                            return v.tensor_mul(stageb[buf][0:rows, :],
                                                tmp1[0:rows, :],
                                                ps[buf][0:rows, :])
                        p.comp("vector", vep, key="vd",
                               wait=(("sd", p.tok("sd")), ("st", stok[buf])))
                    atok[buf] = p.tok("vd")
                    p.dma("scalar", out[mb * 128:mb * 128 + rows, tsl],
                          stageb[buf][0:rows, :], key="st",
                          wait=(("vd", p.tok("vd")),))
                else:
                    raise ValueError(epi)
                stok[buf] = p.tok("st")
                it += 1


def emit_proj0(p, io):
    emit_sweep(p, io, [
        {"klist": [("WQB", "XN0", 2048)], "M": 2048, "out": "QF",
         "epi": "scale:0.08838834764831845"},
        {"klist": [("WKB", "XN0", 2048)], "M": 512, "out": "KF", "epi": "bf16"},
        {"klist": [("WVB", "XN0", 2048)], "M": 512, "out": "VF", "epi": "bf16"},
    ], tb=2048, tag="pj0")


def emit_ffn12(p, io):
    emit_sweep(p, io, [
        {"klist": [("Q1T", "XN0", 2048)], "klist2": [("Q2T", "XN0", 2048)],
         "M": 5632, "out": "HID", "epi": "bitglu"},
    ], tb=1024, tag="f12")


def emit_asm0(p, io):
    emit_sweep(p, io, [
        {"klist": [("Q3T", "HID", 5632), ("WOB", "OA", 2048)],
         "M": 2048, "out": "XF1", "epi": "resid", "aux": "XF0"},
    ], tb=1024, tag="as0")


def emit_proj1(p, io):
    emit_sweep(p, io, [
        {"klist": [("WQKVZT", "XN1", 2048)], "M": 6144, "out": "QKVZ",
         "epi": "bf16"},
        {"klist": [("WBAT", "XN1", 2048)], "M": 32, "out": "BAT",
         "epi": "f32"},
    ], tb=2048, tag="pj1")


def emit_shared(p, io):
    emit_sweep(p, io, [
        {"klist": [("SW1T", "XN1", 2048)], "klist2": [("SWGT", "XN1", 2048)],
         "M": 1024, "out": "HIDS", "epi": "sqreluglu"},
    ], tb=1024, tag="shr")


def emit_asm1(p, io):
    emit_sweep(p, io, [
        {"klist": [("WOUTT", "XATT", 2048), ("SW2T", "HIDS", 1024),
                   ("WUPT", "ACC", 512)],
         "M": 2048, "out": "X2F", "epi": "resid", "aux": "XF1"},
    ], tb=1024, tag="as1")


def emit_xout(p, io):
    nc, es = p.nc, p.es
    x2v = rearr_fp(io["X2F"])
    out = io["OUT"]
    src = [es.enter_context(nc.sbuf_tensor(f"xo_s{i}", [128, 2048], F32))
           for i in range(2)]
    stg = [es.enter_context(nc.sbuf_tensor(f"xo_g{i}", [128, 2048], F32))
           for i in range(2)]
    ps = es.enter_context(nc.psum_tensor("xo_ps", [128, 2048], F32))
    idf = es.enter_context(nc.sbuf_tensor("xo_id", [128, 128], F32))
    p.dma("sync", idf[:], io["IDF32"][:])

    it = 0
    for s in range(B):
        for ti in range(SEQ // 128):
            buf = it % 2
            tcol = s * SEQ + ti * 128
            p.dma("sync", src[buf][:].rearrange("q (fc t) -> q fc t", t=128),
                  x2v[:, :, tcol:tcol + 128], key="ld",
                  wait=(("td", p.tok("td")),))
            ldt = p.tok("ld")
            for g in range(4):
                def tfn(t, buf=buf, g=g):
                    for i in range(4):
                        fc = g * 4 + i
                        inst = t.transpose(
                            ps[:, g * 512 + i * 128: g * 512 + (i + 1) * 128],
                            src[buf][:, fc * 128:(fc + 1) * 128], idf[:])
                    return inst
                p.comp("tensor", tfn, key="td",
                       wait=(("ld", ldt), ("vd", p.tok("vd"))))

                def vfn(v, buf=buf, g=g):
                    return v.tensor_copy(stg[buf][:, g * 512:(g + 1) * 512],
                                         ps[:, g * 512:(g + 1) * 512])
                p.comp("vector", vfn, key="vd",
                       wait=(("td", p.tok("td")), ("st", p.tok("st"))))
            p.dma("scalar", out[s, ti * 128:(ti + 1) * 128, :],
                  stg[buf][:], key="st", wait=(("vd", p.tok("vd")),))
            it += 1


def emit_attn0(p, io):
    nc, es = p.nc, p.es
    NKV, NH = 4, 16
    kfs = es.enter_context(nc.sbuf_tensor("at_kf", [128, NKV * T], BF16))
    vfs = es.enter_context(nc.sbuf_tensor("at_vf", [128, NKV * T], BF16))
    vts = es.enter_context(nc.sbuf_tensor("at_vt", [128, NKV * T], BF16))
    cmsk = es.enter_context(nc.sbuf_tensor("at_cm", [128, 4 * 512], F32))
    idb = es.enter_context(nc.sbuf_tensor("at_id", [128, 128], BF16))
    qhs = [es.enter_context(nc.sbuf_tensor(f"at_q{i}", [128, SEQ], BF16))
           for i in range(2)]
    pbuf = es.enter_context(nc.sbuf_tensor("at_pb", [128, SEQ], BF16))
    pts = es.enter_context(nc.sbuf_tensor("at_pt", [128, 512], BF16))
    ost = [es.enter_context(nc.sbuf_tensor(f"at_o{i}", [128, SEQ], BF16))
           for i in range(2)]
    red = es.enter_context(nc.sbuf_tensor("at_rd", [128, 8], F32))
    psS = es.enter_context(nc.psum_tensor("at_psS", [128, 2048], F32))
    psT = es.enter_context(nc.psum_tensor("at_psT", [128, 512], BF16))
    psO = es.enter_context(nc.psum_tensor("at_psO", [128, 128], F32))

    p.dma("sync", kfs[:].rearrange("q (kv t) -> q kv t", t=T),
          rearr_fp(io["KF"])[:], key="ld")
    p.dma("sync", vfs[:].rearrange("q (kv t) -> q kv t", t=T),
          rearr_fp(io["VF"])[:], key="ld")
    p.dma("sync", cmsk[:].rearrange("q (r k) -> q r k", k=512),
          io["CMASK"].rearrange("r q k -> q r k")[:], key="ld")
    p.dma("gpsimd", idb[:], io["IDBF"][:])
    ld0 = p.tok("ld")

    for kv in range(NKV):
        for s in range(B):
            for grp in range(4):
                def tfn(t, kv=kv, s=s, grp=grp):
                    for i in range(4):
                        ti = grp * 4 + i
                        inst = t.transpose(
                            psT[:, i * 128:(i + 1) * 128],
                            vfs[:, kv * T + s * SEQ + ti * 128:
                                kv * T + s * SEQ + (ti + 1) * 128],
                            idb[:])
                    return inst
                p.comp("tensor", tfn, key="td",
                       wait=(("ld", ld0), ("vd", p.tok("vd"))))

                def vfn(v, kv=kv, s=s, grp=grp):
                    return v.tensor_copy(
                        vts[:, (kv * B + s) * SEQ + grp * 512:
                            (kv * B + s) * SEQ + (grp + 1) * 512],
                        psT[:])
                p.comp("vector", vfn, key="vd", wait=(("td", p.tok("td")),))
    vt_done = p.tok("vd")

    qfv = io["QF"]
    it = 0
    for s in range(B):
        for h in range(NH):
            kv = h // 4
            buf = it % 2
            p.dma("sync", qhs[buf][:],
                  qfv[h * 128:(h + 1) * 128, s * SEQ:(s + 1) * SEQ],
                  key="ld", wait=(("td", p.tok("td")),))
            qld = p.tok("ld")
            for qi in range(SEQ // 128):
                nkc = qi // 4 + 1
                kbase = kv * T + s * SEQ

                def t_qk(t, buf=buf, qi=qi, nkc=nkc, kbase=kbase):
                    for kc in range(nkc):
                        inst = t.matmul(
                            psS[:, kc * 512:(kc + 1) * 512],
                            qhs[buf][:, qi * 128:(qi + 1) * 128],
                            kfs[:, kbase + kc * 512: kbase + (kc + 1) * 512],
                            start=True, stop=True)
                    return inst
                p.comp("tensor", t_qk, key="td",
                       wait=(("ld", qld), ("vd", p.tok("vd")),
                             ("sd", p.tok("sd"))))
                qkt = p.tok("td")

                def v_mask(v, qi=qi, nkc=nkc):
                    dk = (nkc - 1) * 512
                    v.tensor_add(psS[:, dk:dk + 512], psS[:, dk:dk + 512],
                                 cmsk[:, (qi % 4) * 512:(qi % 4 + 1) * 512])
                    return v.reduce_max(red[:, 0:1], psS[:, 0:nkc * 512],
                                        axis=AX.X, negate=True)
                p.comp("vector", v_mask, key="vd", wait=(("td", qkt),))
                mskt = p.tok("vd")

                def s_exp(s_, qi=qi, nkc=nkc):
                    for kc in range(nkc):
                        inst = s_.activation(
                            pbuf[:, kc * 512:(kc + 1) * 512],
                            psS[:, kc * 512:(kc + 1) * 512], AF.Exp,
                            bias=red[:, 0:1],
                            accum_out=red[:, 1 + kc:2 + kc])
                    return inst
                p.comp("scalar", s_exp, key="sd", wait=(("vd", mskt),))
                expt = p.tok("sd")

                def v_lsum(v, nkc=nkc):
                    return v.reduce_sum(red[:, 5:6], red[:, 1:1 + nkc],
                                        axis=AX.X)
                p.comp("vector", v_lsum, key="vd", wait=(("sd", expt),))

                def v_rec(v):
                    return v.reciprocal(red[:, 6:7], red[:, 5:6])
                p.comp("vector", v_rec, key="vd")

                def v_norm(v, nkc=nkc):
                    return v.tensor_scalar_mul(pbuf[:, 0:nkc * 512],
                                               pbuf[:, 0:nkc * 512],
                                               red[:, 6:7])
                p.comp("vector", v_norm, key="vd")
                pnt = p.tok("vd")

                for kc in range(nkc):
                    def t_tr(t, kc=kc):
                        for i in range(4):
                            inst = t.transpose(
                                psT[:, i * 128:(i + 1) * 128],
                                pbuf[:, kc * 512 + i * 128:
                                     kc * 512 + (i + 1) * 128], idb[:])
                        return inst
                    p.comp("tensor", t_tr, key="td",
                           wait=(("vd", pnt if kc == 0 else p.tok("vd")),))

                    def v_pt(v):
                        return v.tensor_copy(pts[:], psT[:])
                    p.comp("vector", v_pt, key="vd",
                           wait=(("td", p.tok("td")),))
                    ptt = p.tok("vd")

                    def t_pv(t, kc=kc, nkc=nkc, kv=kv, s=s):
                        for i in range(4):
                            ki = kc * 4 + i
                            inst = t.matmul(
                                psO[:],
                                vts[:, (kv * B + s) * SEQ + ki * 128:
                                    (kv * B + s) * SEQ + (ki + 1) * 128],
                                pts[:, i * 128:(i + 1) * 128],
                                start=(ki == 0), stop=(ki == nkc * 4 - 1))
                        return inst
                    p.comp("tensor", t_pv, key="td", wait=(("vd", ptt),))
                pvt = p.tok("td")

                def v_o(v, buf=buf, qi=qi):
                    return v.tensor_copy(ost[buf][:, qi * 128:(qi + 1) * 128],
                                         psO[:])
                p.comp("vector", v_o, key="vd",
                       wait=(("td", pvt), ("st", p.tok("st"))))
            p.dma("scalar", io["OA"][h * 128:(h + 1) * 128,
                                     s * SEQ:(s + 1) * SEQ],
                  ost[buf][:], key="st", wait=(("vd", p.tok("vd")),))
            it += 1


def emit_conv(p, io):
    nc, es = p.nc, p.es
    NKH, NVH = 8, 16
    QSCALE = 0.08838834764831845

    b16 = es.enter_context(nc.sbuf_tensor("cv_b16", [16, T], F32))
    a16 = es.enter_context(nc.sbuf_tensor("cv_a16", [16, T], F32))
    beta = es.enter_context(nc.sbuf_tensor("cv_beta", [16, T], F32))
    grow = es.enter_context(nc.sbuf_tensor("cv_g", [16, T], F32))
    crow = es.enter_context(nc.sbuf_tensor("cv_c", [16, T], F32))
    trow = [es.enter_context(nc.sbuf_tensor(f"cv_t{i}", [16, T], F32))
            for i in range(2)]
    zrow = es.enter_context(nc.sbuf_tensor("cv_z", [16, 128], F32))
    alog = es.enter_context(nc.sbuf_tensor("cv_al", [16, 1], F32))
    dtb = es.enter_context(nc.sbuf_tensor("cv_dt", [16, 1], F32))
    nega = es.enter_context(nc.sbuf_tensor("cv_na", [16, 1], F32))
    cws = es.enter_context(nc.sbuf_tensor("cv_cw", [128, 32 * 4], F32))
    ones = es.enter_context(nc.sbuf_tensor("cv_on", [128, 1], F32))
    ones1 = es.enter_context(nc.sbuf_tensor("cv_o1", [1, 128], F32))
    itile = [es.enter_context(nc.sbuf_tensor(f"cv_i{i}", [128, 515], BF16))
             for i in range(2)]
    acc = [es.enter_context(nc.sbuf_tensor(f"cv_ac{i}", [128, 512], F32))
           for i in range(2)]
    co = es.enter_context(nc.sbuf_tensor("cv_co", [128, 512], BF16))
    sq = es.enter_context(nc.sbuf_tensor("cv_sq", [128, 512], F32))
    rs = es.enter_context(nc.sbuf_tensor("cv_rs", [1, 512], F32))
    outt = [es.enter_context(nc.sbuf_tensor(f"cv_ot{i}", [128, 512], BF16))
            for i in range(2)]
    psn = es.enter_context(nc.psum_tensor("cv_pn", [1, 512], F32))
    psb = es.enter_context(nc.psum_tensor("cv_pb", [128, 512], F32))

    for v in range(16):
        g_, r_ = v // 2, v % 2
        p.dma("sync", b16[v:v + 1, :], io["BAT"][4 * g_ + r_:4 * g_ + r_ + 1, :],
              key="ld")
        p.dma("sync", a16[v:v + 1, :],
              io["BAT"][4 * g_ + 2 + r_:4 * g_ + 3 + r_, :], key="ld")
    p.dma("sync", alog[:], io["L1_attn_A_log"][:], key="ld")
    p.dma("sync", dtb[:], io["L1_attn_dt_bias"][:], key="ld")
    p.dma("sync", cws[:].rearrange("q (rc j) -> q rc j", j=4),
          io["L1_attn_conv_w"].rearrange("(rc q) j -> q rc j", q=128)[:],
          key="ld")
    p.dma("sync", ones[:], io["ONES128"][:], key="ld")
    p.dma("sync", ones1[:], io["ONES1"][:], key="ld")
    ld0 = p.tok("ld")

    def s_beta(s):
        s.activation(nega[:], alog[:], AF.Exp)
        return s.activation(beta[:], b16[:], AF.Sigmoid)
    p.comp("scalar", s_beta, key="sd", wait=(("ld", ld0),))

    def v_dt(v):
        return v.tensor_scalar_add(trow[0][:], a16[:], dtb[:, 0:1])
    p.comp("vector", v_dt, key="vd", wait=(("ld", ld0),))

    def s_sp1(s):
        s.activation(trow[1][:], trow[0][:], AF.Abs)
        return s.activation(trow[1][:], trow[1][:], AF.Exp, scale=-1.0)
    p.comp("scalar", s_sp1, key="sd", wait=(("vd", p.tok("vd")),))

    def s_sp2(s):
        return s.activation(trow[1][:], trow[1][:], AF.Ln, bias=1.0)
    p.comp("scalar", s_sp2, key="sd")
    spt = p.tok("sd")

    def v_g(v):
        v.tensor_scalar_mul(nega[:], nega[:], -1.0)
        v.tensor_scalar_max(trow[0][:], trow[0][:], 0.0)
        v.tensor_add(grow[:], trow[0][:], trow[1][:])
        return v.tensor_scalar_mul(grow[:], grow[:], nega[:, 0:1])
    p.comp("vector", v_g, key="vd", wait=(("sd", spt),))

    def v_c(v):
        v.memset(trow[0][:], 0.0)
        for n in range(T // 128):
            inst = v.tensor_tensor_scan(
                crow[:, n * 128:(n + 1) * 128],
                grow[:, n * 128:(n + 1) * 128],
                trow[0][:, n * 128:(n + 1) * 128],
                0.0, ALU.add, ALU.add)
        return inst
    p.comp("vector", v_c, key="vd")
    bad = p.tok("vd")
    p.dma("scalar", io["CBE"][0:16, :], crow[:], key="st", wait=(("vd", bad),))
    p.dma("scalar", io["CBE"][16:32, :], beta[:], key="st", wait=(("vd", bad),))

    qkvz = io["QKVZ"]

    def srcrow(rc):
        if rc < 8:
            return rc * 768
        if rc < 16:
            return (rc - 8) * 768 + 128
        vh = rc - 16
        return (vh // 2) * 768 + 256 + (vh % 2) * 128

    it = 0
    for rc in range(32):
        base = srcrow(rc)
        kind = "q" if rc < 8 else ("k" if rc < 16 else "v")
        for s in range(B):
            for tci in range(SEQ // 512):
                buf = it % 2
                c0 = s * SEQ + tci * 512

                def v_zero(v, buf=buf):
                    return v.memset(itile[buf][:, 0:3], 0.0)
                if tci == 0:
                    p.comp("vector", v_zero, key="vd",
                           wait=(("td", p.tok("td")), ("sd", p.tok("sd"))))
                    p.dma("sync", itile[buf][:, 3:515],
                          qkvz[base:base + 128, c0:c0 + 512], key="ld",
                          wait=(("vd", p.tok("vd")),))
                else:
                    p.dma("sync", itile[buf][:, 0:515],
                          qkvz[base:base + 128, c0 - 3:c0 + 512], key="ld",
                          wait=(("td", p.tok("td")), ("sd", p.tok("sd")),
                                ("vd", p.tok("vd"))))
                ldt = p.tok("ld")

                def v_conv(v, buf=buf, rc=rc):
                    v.tensor_scalar_mul(acc[0][:], itile[buf][:, 0:512],
                                        cws[:, rc * 4:rc * 4 + 1])
                    v.scalar_tensor_tensor(acc[1][:], itile[buf][:, 1:513],
                                           cws[:, rc * 4 + 1:rc * 4 + 2],
                                           acc[0][:], ALU.mult, ALU.add)
                    v.scalar_tensor_tensor(acc[0][:], itile[buf][:, 2:514],
                                           cws[:, rc * 4 + 2:rc * 4 + 3],
                                           acc[1][:], ALU.mult, ALU.add)
                    return v.scalar_tensor_tensor(
                        acc[1][:], itile[buf][:, 3:515],
                        cws[:, rc * 4 + 3:rc * 4 + 4],
                        acc[0][:], ALU.mult, ALU.add)
                p.comp("vector", v_conv, key="vd",
                       wait=(("ld", ldt), ("sd", p.tok("sd"))))
                cvt = p.tok("vd")

                def s_silu(s_):
                    return s_.activation(co[:], acc[1][:], AF.Silu)
                p.comp("scalar", s_silu, key="sd", wait=(("vd", cvt),))
                silt = p.tok("sd")

                if kind == "v":
                    def v_out(v, buf=buf):
                        return v.tensor_copy(outt[buf][:], co[:])
                    p.comp("vector", v_out, key="vd",
                           wait=(("sd", silt), ("st", p.tok("st"))))
                    p.dma("scalar",
                          io["VC"][(rc - 16) * 128:(rc - 15) * 128,
                                   c0:c0 + 512],
                          outt[buf][:], key="st", wait=(("vd", p.tok("vd")),))
                else:
                    def s_sq(s_):
                        return s_.activation(sq[:], co[:], AF.Square)
                    p.comp("scalar", s_sq, key="sd")
                    sqt = p.tok("sd")

                    def t_n(t):
                        return t.matmul(psn[0:1, :], ones[:], sq[:],
                                        start=True, stop=True)
                    p.comp("tensor", t_n, key="td",
                           wait=(("sd", sqt), ("vd", p.tok("vd"))))

                    def v_eps(v):
                        return v.tensor_scalar_add(rs[0:1, :], psn[0:1, :],
                                                   1e-6)
                    p.comp("vector", v_eps, key="vd",
                           wait=(("td", p.tok("td")),))

                    def s_rt(s_):
                        return s_.activation(rs[0:1, :], rs[0:1, :], AF.Sqrt)
                    p.comp("scalar", s_rt, key="sd",
                           wait=(("vd", p.tok("vd")),))

                    def v_rc(v):
                        return v.reciprocal(rs[0:1, :], rs[0:1, :])
                    p.comp("vector", v_rc, key="vd", wait=(("sd", p.tok("sd")),))

                    def t_bc(t):
                        return t.matmul(psb[:], ones1[:], rs[0:1, :],
                                        start=True, stop=True)
                    p.comp("tensor", t_bc, key="td", wait=(("vd", p.tok("vd")),))
                    bct = p.tok("td")

                    if kind == "q":
                        def v_out(v, buf=buf):
                            return v.scalar_tensor_tensor(
                                outt[buf][:], co[:], QSCALE, psb[:],
                                ALU.mult, ALU.mult)
                    else:
                        def v_out(v, buf=buf):
                            return v.tensor_mul(outt[buf][:], co[:], psb[:])
                    p.comp("vector", v_out, key="vd",
                           wait=(("td", bct), ("st", p.tok("st"))))
                    dst = io["QN"] if kind == "q" else io["KN"]
                    g = rc if rc < 8 else rc - 8
                    p.dma("scalar", dst[g * 128:(g + 1) * 128, c0:c0 + 512],
                          outt[buf][:], key="st", wait=(("vd", p.tok("vd")),))
                it += 1

    for vh in range(NVH):
        zbase = (vh // 2) * 768 + 512 + (vh % 2) * 128
        for s in range(B):
            for tci in range(SEQ // 512):
                buf = it % 2
                c0 = s * SEQ + tci * 512
                p.dma("sync", itile[buf][:, 0:512],
                      qkvz[zbase:zbase + 128, c0:c0 + 512], key="ld",
                      wait=(("sd", p.tok("sd")),))
                ldt = p.tok("ld")

                def s_z(s_, buf=buf):
                    return s_.activation(outt[buf][:], itile[buf][:, 0:512],
                                         AF.Silu)
                p.comp("scalar", s_z, key="sd",
                       wait=(("ld", ldt), ("st", p.tok("st"))))
                p.dma("scalar", io["ZS"][vh * 128:(vh + 1) * 128, c0:c0 + 512],
                      outt[buf][:], key="st", wait=(("sd", p.tok("sd")),))
                it += 1


def emit_gatex(p, io):
    emit_sweep(p, io, [
        {"klist": [("WDOWNT", "XN1", 2048)], "M": 512, "out": "XL",
         "epi": "bf16"},
    ], tb=2048, tag="gtx")


def emit_gate(p, io):
    nc, es = p.nc, p.es
    xblk = es.enter_context(nc.sbuf_tensor("gt_x", [128, 16 * 2048], BF16))
    wg = es.enter_context(nc.sbuf_tensor("gt_w", [128, 16 * 16], BF16))
    prb = es.enter_context(nc.sbuf_tensor("gt_pr", [16, 2048], F32))
    ptm = es.enter_context(nc.sbuf_tensor("gt_pt", [128, 16], F32))
    yy = es.enter_context(nc.sbuf_tensor("gt_y", [128, 16], F32))
    rk = [es.enter_context(nc.sbuf_tensor(f"gt_rk{i}", [128, 16], F32))
          for i in range(2)]
    bb = es.enter_context(nc.sbuf_tensor("gt_bb", [128, 16], F32))
    brow = es.enter_context(nc.sbuf_tensor("gt_br", [1, 16], F32))
    den = es.enter_context(nc.sbuf_tensor("gt_dn", [128, 2], F32))
    wds = es.enter_context(nc.sbuf_tensor("gt_wd", [16, 2048], F32))
    idf = es.enter_context(nc.sbuf_tensor("gt_id", [128, 128], F32))
    ones1 = es.enter_context(nc.sbuf_tensor("gt_o1", [1, 128], F32))
    psg = es.enter_context(nc.psum_tensor("gt_ps", [16, 2048], F32))
    pst = es.enter_context(nc.psum_tensor("gt_pt2", [128, 16], F32))
    psb = es.enter_context(nc.psum_tensor("gt_pb", [128, 16], F32))
    psw = es.enter_context(nc.psum_tensor("gt_pw", [16, 128], F32))

    p.dma("sync", idf[:], io["IDF32"][:], key="ld")
    p.dma("sync", ones1[:], io["ONES1"][:], key="ld")
    p.dma("sync", brow[0:1, :], io["L1_ffn_biases"][:], key="ld")
    p.dma("sync", wg[:].rearrange("q (kc m) -> q kc m", m=16),
          io["WGT"].rearrange("(kc q) m -> q kc m", q=128)[:], key="ld")
    ld0 = p.tok("ld")

    def t_bb(t):
        return t.matmul(psb[:], ones1[:], brow[0:1, :], start=True, stop=True)
    p.comp("tensor", t_bb, key="td", wait=(("ld", ld0),))

    def v_bb(v):
        return v.tensor_copy(bb[:], psb[:])
    p.comp("vector", v_bb, key="vd", wait=(("td", p.tok("td")),))

    for tbi in range(T // 2048):
        tsl = slice(tbi * 2048, (tbi + 1) * 2048)
        p.dma("sync", xblk[:].rearrange("q (kc t) -> q kc t", t=2048),
              rearr_fp(io["XN1"])[:, :, tsl], key="ld",
              wait=(("td", p.tok("td")),))
        xld = p.tok("ld")

        def t_pr(t):
            for kc in range(16):
                for nb in range(4):
                    inst = t.matmul(psg[0:16, nb * 512:(nb + 1) * 512],
                                    wg[:, kc * 16:(kc + 1) * 16],
                                    xblk[:, kc * 2048 + nb * 512:
                                         kc * 2048 + (nb + 1) * 512],
                                    start=(kc == 0), stop=(kc == 15))
            return inst
        p.comp("tensor", t_pr, key="td",
               wait=(("ld", xld), ("sd", p.tok("sd"))))

        def s_sig(s_):
            return s_.activation(prb[:], psg[0:16, :], AF.Sigmoid)
        p.comp("scalar", s_sig, key="sd", wait=(("td", p.tok("td")),))
        sigt = p.tok("sd")

        for i in range(16):
            def t_tr(t, i=i):
                return t.transpose(pst[:, 0:16], prb[:, i * 128:(i + 1) * 128],
                                   idf[0:16, 0:16])
            p.comp("tensor", t_tr, key="td",
                   wait=(("sd", sigt), ("vd", p.tok("vd"))))

            p.comp("vector", lambda v: v.tensor_copy(ptm[:], pst[:, 0:16]),
                   key="vd", wait=(("td", p.tok("td")),))
            p.comp("vector", lambda v: v.tensor_add(yy[:], ptm[:], bb[:]),
                   key="vd")
            p.comp("vector", lambda v: v.tensor_single_scalar(
                rk[0][:], yy[:], yy[:, 0:1], ALU.is_lt), key="vd")
            for j in range(1, 16):
                p.comp("vector", lambda v, j=j: v.scalar_tensor_tensor(
                    rk[j % 2][:], yy[:], yy[:, j:j + 1], rk[(j + 1) % 2][:],
                    ALU.is_lt, ALU.add), key="vd")
            p.comp("vector", lambda v: v.tensor_single_scalar(
                rk[0][:], rk[1][:], 7.5, ALU.is_le), key="vd")
            p.comp("vector", lambda v: v.tensor_mul(rk[1][:], ptm[:], rk[0][:]),
                   key="vd")
            p.comp("vector", lambda v: v.reduce_sum(den[:, 0:1], rk[1][:],
                                                    axis=AX.X), key="vd")
            p.comp("vector", lambda v: v.reciprocal(den[:, 1:2], den[:, 0:1]),
                   key="vd")
            p.comp("vector", lambda v: v.tensor_scalar(
                rk[0][:], rk[1][:], den[:, 1:2], 2.5, ALU.mult, ALU.mult),
                key="vd")

            def t_wt(t, i=i):
                return t.transpose(psw[0:16, :], rk[0][:], idf[:])
            p.comp("tensor", t_wt, key="td", wait=(("vd", p.tok("vd")),))

            def v_wd(v, i=i):
                return v.tensor_copy(wds[:, i * 128:(i + 1) * 128],
                                     psw[0:16, :])
            p.comp("vector", v_wd, key="vd",
                   wait=(("td", p.tok("td")), ("st", p.tok("st"))))
        p.dma("scalar", io["WD"][:, tsl], wds[:], key="st",
              wait=(("vd", p.tok("vd")),))


def emit_scan(p, io):
    nc, es = p.nc, p.es
    NKH, NVH, C = 8, 16, 128
    NCH = SEQ // C

    qn = es.enter_context(nc.sbuf_tensor("sc_q", [128, NKH * C], BF16))
    kn = es.enter_context(nc.sbuf_tensor("sc_k", [128, NKH * C], BF16))
    vc = es.enter_context(nc.sbuf_tensor("sc_v", [128, NVH * C], BF16))
    zs = es.enter_context(nc.sbuf_tensor("sc_z", [128, NVH * C], BF16))
    cbe = es.enter_context(nc.sbuf_tensor("sc_cb", [32, C], F32))
    ccol = es.enter_context(nc.sbuf_tensor("sc_cc", [128, 16], F32))
    bcol = es.enter_context(nc.sbuf_tensor("sc_bc", [128, 16], F32))
    gcol = es.enter_context(nc.sbuf_tensor("sc_gc", [128, 16], F32))
    cbcol = es.enter_context(nc.sbuf_tensor("sc_cbc", [128, 16], F32))
    nccol = es.enter_context(nc.sbuf_tensor("sc_nc", [128, 16], F32))
    bgcol = es.enter_context(nc.sbuf_tensor("sc_bg", [128, 16], F32))
    gccol = es.enter_context(nc.sbuf_tensor("sc_gC", [128, 16], F32))
    dccol = es.enter_context(nc.sbuf_tensor("sc_dc", [128, 16], F32))
    ccr = es.enter_context(nc.sbuf_tensor("sc_ccr", [1, 16], F32))
    mpos = es.enter_context(nc.sbuf_tensor("sc_mp", [128, 128], F32))
    minc = es.enter_context(nc.sbuf_tensor("sc_mi", [128, 128], F32))
    eh = es.enter_context(nc.sbuf_tensor("sc_eh", [16, 16 * 128], F32))
    idf = es.enter_context(nc.sbuf_tensor("sc_id", [128, 128], F32))
    idb = es.enter_context(nc.sbuf_tensor("sc_idb", [128, 128], BF16))
    ones1 = es.enter_context(nc.sbuf_tensor("sc_o1", [1, 128], F32))
    nwbc = es.enter_context(nc.sbuf_tensor("sc_nw", [128, 128], F32))
    nwrow = es.enter_context(nc.sbuf_tensor("sc_nwr", [1, 128], F32))
    sfp = es.enter_context(nc.sbuf_tensor("sc_sf", [128, NVH * 128], F32))
    sbf = es.enter_context(nc.sbuf_tensor("sc_sb", [128, NVH * 128], BF16))
    ktm = es.enter_context(nc.sbuf_tensor("sc_kt", [128, NKH * 128], BF16))
    kbt = es.enter_context(nc.sbuf_tensor("sc_kb", [128, 128], BF16))
    cbc = es.enter_context(nc.sbuf_tensor("sc_cbc2", [128, 128], F32))
    tmsk = es.enter_context(nc.sbuf_tensor("sc_tm", [128, 2 * 128], F32))
    dexp = es.enter_context(nc.sbuf_tensor("sc_dx", [128, 128], F32))
    aqt = es.enter_context(nc.sbuf_tensor("sc_aq", [128, 128], BF16))
    lsb = es.enter_context(nc.sbuf_tensor("sc_l", [128, 128], BF16))
    ltsb = es.enter_context(nc.sbuf_tensor("sc_lt", [128, 128], BF16))
    pwr = es.enter_context(nc.sbuf_tensor("sc_pw", [128, 2 * 128], BF16))
    ptl = es.enter_context(nc.sbuf_tensor("sc_ptl", [128, 6 * 128], BF16))
    rsb = es.enter_context(nc.sbuf_tensor("sc_r", [128, 128], BF16))
    xsb = es.enter_context(nc.sbuf_tensor("sc_x", [128, 2 * 128], BF16))
    t1 = es.enter_context(nc.sbuf_tensor("sc_t1", [128, 128], F32))
    otm = es.enter_context(nc.sbuf_tensor("sc_ot", [128, 128], F32))
    xfg = es.enter_context(nc.sbuf_tensor("sc_xf", [128, NVH * 128], F32))
    ssq = es.enter_context(nc.sbuf_tensor("sc_sq", [128, NVH], F32))
    sqv = es.enter_context(nc.sbuf_tensor("sc_sqv", [128, 128], F32))
    rscol = es.enter_context(nc.sbuf_tensor("sc_rs", [128, 16], F32))
    xnrm = es.enter_context(nc.sbuf_tensor("sc_xn", [128, NVH * 128], BF16))
    stg = es.enter_context(nc.sbuf_tensor("sc_st", [128, NVH * 128], BF16))
    pgram = es.enter_context(nc.psum_tensor("sc_pg", [128, 2 * 128], F32))
    pmisc = es.enter_context(nc.psum_tensor("sc_pm", [128, 512], F32))
    psol = es.enter_context(nc.psum_tensor("sc_ps", [128, 2 * 128], F32))
    ptr = es.enter_context(nc.psum_tensor("sc_ptr", [128, 512], BF16))
    pr32 = es.enter_context(nc.psum_tensor("sc_pr32", [128, 256], F32))

    p.dma("sync", mpos[:], io["MASKS_POS"][:], key="ld")
    p.dma("sync", minc[:], io["MASKU_INC"][:], key="ld")
    p.dma("sync", eh[:].rearrange("q (h m) -> q h m", m=128),
          io["EH"].rearrange("h q m -> q h m")[:], key="ld")
    p.dma("sync", idf[:], io["IDF32"][:], key="ld")
    p.dma("gpsimd", idb[:], io["IDBF"][:], key="ld")
    p.dma("sync", ones1[:], io["ONES1"][:], key="ld")
    p.dma("sync", nwrow[0:1, :], io["L1_attn_norm_weight"][:], key="ld")
    ld0 = p.tok("ld")

    def t_nw(t):
        return t.matmul(pmisc[:, 0:128], ones1[:], nwrow[0:1, :],
                        start=True, stop=True)
    p.comp("tensor", t_nw, key="td", wait=(("ld", ld0),))

    def v_nw(v):
        return v.tensor_copy(nwbc[:], pmisc[:, 0:128])
    p.comp("vector", v_nw, key="vd", wait=(("td", p.tok("td")),))

    for s in range(B):
        def v_s0(v):
            v.memset(sfp[:], 0.0)
            return v.memset(sbf[:], 0.0)
        p.comp("vector", v_s0, key="vd", wait=(("td", p.tok("td")),))

        for n in range(NCH):
            csl = slice(s * SEQ + n * C, s * SEQ + (n + 1) * C)
            ldw = (("td", p.tok("td")), ("vd", p.tok("vd")))
            p.dma("sync", qn[:].rearrange("q (g t) -> q g t", t=C),
                  rearr_fp(io["QN"])[:, :, csl], key="ld", wait=ldw)
            p.dma("sync", kn[:].rearrange("q (g t) -> q g t", t=C),
                  rearr_fp(io["KN"])[:, :, csl], key="ld")
            p.dma("sync", vc[:].rearrange("q (h t) -> q h t", t=C),
                  rearr_fp(io["VC"])[:, :, csl], key="ld")
            p.dma("sync", zs[:].rearrange("q (h t) -> q h t", t=C),
                  rearr_fp(io["ZS"])[:, :, csl], key="ld")
            p.dma("sync", cbe[:], io["CBE"][:, csl], key="ld")
            ldt = p.tok("ld")

            def t_cb(t):
                return t.transpose(pmisc[:, 0:32], cbe[:], idf[0:32, 0:32])
            p.comp("tensor", t_cb, key="td",
                   wait=(("ld", ldt), ("vd", p.tok("vd"))))

            def v_cb(v):
                v.tensor_copy(ccol[:], pmisc[:, 0:16])
                return v.tensor_copy(bcol[:], pmisc[:, 16:32])
            p.comp("vector", v_cb, key="vd", wait=(("td", p.tok("td")),))

            def t_ccr(t):
                return t.transpose(pmisc[0:1, 32:48], cbe[0:16, C - 1:C],
                                   idf[0:16, 0:16])
            p.comp("tensor", t_ccr, key="td", wait=(("vd", p.tok("vd")),))

            def v_ccr(v):
                return v.tensor_copy(ccr[0:1, :], pmisc[0:1, 32:48])
            p.comp("vector", v_ccr, key="vd", wait=(("td", p.tok("td")),))

            def t_bcc(t):
                return t.matmul(pmisc[:, 64:80], ones1[:], ccr[0:1, :],
                                start=True, stop=True)
            p.comp("tensor", t_bcc, key="td", wait=(("vd", p.tok("vd")),))
            bcct = p.tok("td")

            def s_prep(s_):
                s_.activation(gcol[:], ccol[:], AF.Exp)
                s_.activation(cbcol[:], bcol[:], AF.Ln)
                return s_.activation(gccol[:], pmisc[:, 64:80], AF.Exp)
            p.comp("scalar", s_prep, key="sd", wait=(("td", bcct),))

            def v_prep(v):
                v.tensor_add(cbcol[:], cbcol[:], ccol[:])
                v.tensor_scalar_mul(nccol[:], ccol[:], -1.0)
                v.tensor_mul(bgcol[:], bcol[:], gcol[:])
                return v.tensor_sub(dccol[:], pmisc[:, 64:80], ccol[:])
            p.comp("vector", v_prep, key="vd", wait=(("sd", p.tok("sd")),))

            def s_dc(s_):
                return s_.activation(dccol[:], dccol[:], AF.Exp)
            p.comp("scalar", s_dc, key="sd", wait=(("vd", p.tok("vd")),))
            prept = p.tok("sd")

            for g in range(NKH):
                def t_kt(t, g=g):
                    return t.transpose(ptr[:, 0:128],
                                       kn[:, g * C:(g + 1) * C], idb[:])
                p.comp("tensor", t_kt, key="td",
                       wait=(("ld", ldt), ("vd", p.tok("vd"))))

                def v_kt(v, g=g):
                    return v.tensor_copy(ktm[:, g * 128:(g + 1) * 128],
                                         ptr[:, 0:128])
                p.comp("vector", v_kt, key="vd", wait=(("td", p.tok("td")),))
            ktt = p.tok("vd")

            for h in range(NVH):
                g = h // 2
                def t_gram(t, g=g):
                    t.matmul(pgram[:, 0:128], kn[:, g * C:(g + 1) * C],
                             kn[:, g * C:(g + 1) * C], start=True, stop=True)
                    return t.matmul(pgram[:, 128:256], kn[:, g * C:(g + 1) * C],
                                    qn[:, g * C:(g + 1) * C],
                                    start=True, stop=True)
                p.comp("tensor", t_gram, key="td",
                       wait=(("sd", prept), ("vd", p.tok("vd"))))

                def t_cbc(t, h=h):
                    return t.matmul(pmisc[:, 128:256], eh[:, h * 128:(h + 1) * 128],
                                    cbe[0:16, :], start=True, stop=True)
                p.comp("tensor", t_cbc, key="td", wait=(("vd", p.tok("vd")),))
                cbct = p.tok("td")

                def v_cbc(v):
                    v.tensor_copy(cbc[:], pmisc[:, 128:256])
                    v.tensor_add(tmsk[:, 0:128], cbc[:], mpos[:])
                    return v.tensor_add(tmsk[:, 128:256], cbc[:], minc[:])
                p.comp("vector", v_cbc, key="vd", wait=(("td", cbct),))
                cbcv = p.tok("vd")

                def s_exp(s_, h=h):
                    s_.activation(dexp[:], tmsk[:, 128:256], AF.Exp,
                                  bias=nccol[:, h:h + 1])
                    return s_.activation(tmsk[:, 0:128], tmsk[:, 0:128], AF.Exp,
                                         scale=-1.0, bias=cbcol[:, h:h + 1])
                p.comp("scalar", s_exp, key="sd", wait=(("vd", cbcv),))
                expt = p.tok("sd")

                def v_lm(v):
                    v.tensor_mul(aqt[:], pgram[:, 128:256], dexp[:])
                    return v.tensor_mul(lsb[:], pgram[:, 0:128], tmsk[:, 0:128])
                p.comp("vector", v_lm, key="vd", wait=(("sd", expt),))
                lmt = p.tok("vd")

                def t_lt(t):
                    return t.transpose(ptr[:, 0:128], lsb[:], idb[:])
                p.comp("tensor", t_lt, key="td", wait=(("vd", lmt),))

                def v_lt(v):
                    return v.tensor_copy(ltsb[:], ptr[:, 0:128])
                p.comp("vector", v_lt, key="vd", wait=(("td", p.tok("td")),))

                def t_r(t, g=g, h=h):
                    t.matmul(pr32[:, 0:128], kn[:, g * C:(g + 1) * C],
                             sbf[:, h * 128:(h + 1) * 128],
                             start=True, stop=True)
                    return t.transpose(ptr[:, 128:256],
                                       vc[:, h * C:(h + 1) * C], idb[:])
                p.comp("tensor", t_r, key="td", wait=(("vd", p.tok("vd")),))
                rt = p.tok("td")

                def v_r(v, h=h):
                    v.tensor_scalar_mul(t1[:], pr32[:, 0:128], bgcol[:, h:h + 1])
                    return v.scalar_tensor_tensor(rsb[:], ptr[:, 128:256],
                                                  bcol[:, h:h + 1], t1[:],
                                                  ALU.mult, ALU.subtract)
                p.comp("vector", v_r, key="vd", wait=(("td", rt),))
                rvt = p.tok("vd")

                def v_p0(v):
                    v.tensor_copy(pwr[:, 0:128], lsb[:])
                    return v.tensor_copy(ptl[:, 0:128], ltsb[:])
                p.comp("vector", v_p0, key="vd")
                for lvl in range(1, 6):
                    def t_sq(t, lvl=lvl):
                        t.matmul(psol[:, 0:128], pwr[:, 0:128],
                                 ptl[:, (lvl - 1) * 128:lvl * 128],
                                 start=True, stop=True)
                        return t.matmul(psol[:, 128:256],
                                        ptl[:, (lvl - 1) * 128:lvl * 128],
                                        pwr[:, 0:128], start=True, stop=True)
                    p.comp("tensor", t_sq, key="td", wait=(("vd", p.tok("vd")),))

                    def v_sq(v, lvl=lvl):
                        v.tensor_copy(ptl[:, lvl * 128:(lvl + 1) * 128],
                                      psol[:, 0:128])
                        return v.tensor_copy(pwr[:, 0:128], psol[:, 128:256])
                    p.comp("vector", v_sq, key="vd", wait=(("td", p.tok("td")),))

                def t_sq6(t):
                    return t.matmul(psol[:, 0:128], pwr[:, 0:128],
                                    ptl[:, 5 * 128:6 * 128],
                                    start=True, stop=True)
                p.comp("tensor", t_sq6, key="td", wait=(("vd", p.tok("vd")),))

                def v_sq6(v):
                    return v.tensor_copy(pwr[:, 128:256], psol[:, 0:128])
                p.comp("vector", v_sq6, key="vd", wait=(("td", p.tok("td")),))

                def v_x0(v):
                    return v.tensor_copy(xsb[:, 0:128], rsb[:])
                p.comp("vector", v_x0, key="vd", wait=(("vd", rvt),))
                xcur = 0
                for ai in range(7):
                    if ai < 6:
                        lvl = 6 - ai
                        def t_ap(t, lvl=lvl, xcur=xcur):
                            src = (pwr[:, 128:256] if lvl == 6
                                   else ptl[:, lvl * 128:(lvl + 1) * 128])
                            return t.matmul(psol[:, 128:256], src,
                                            xsb[:, xcur * 128:(xcur + 1) * 128],
                                            start=True, stop=True)
                        sign = 1.0
                    else:
                        def t_ap(t, xcur=xcur):
                            return t.matmul(psol[:, 128:256], ltsb[:],
                                            xsb[:, xcur * 128:(xcur + 1) * 128],
                                            start=True, stop=True)
                        sign = -1.0
                    p.comp("tensor", t_ap, key="td", wait=(("vd", p.tok("vd")),))

                    def v_ap(v, xcur=xcur, sign=sign):
                        return v.scalar_tensor_tensor(
                            xsb[:, (1 - xcur) * 128:(2 - xcur) * 128],
                            psol[:, 128:256], sign,
                            xsb[:, xcur * 128:(xcur + 1) * 128],
                            ALU.mult, ALU.add)
                    p.comp("vector", v_ap, key="vd", wait=(("td", p.tok("td")),))
                    xcur = 1 - xcur

                def t_o(t, g=g, h=h, xcur=xcur):
                    t.matmul(pr32[:, 128:256], qn[:, g * C:(g + 1) * C],
                             sbf[:, h * 128:(h + 1) * 128],
                             start=True, stop=True)
                    return t.matmul(psol[:, 0:128], aqt[:],
                                    xsb[:, xcur * 128:(xcur + 1) * 128],
                                    start=True, stop=True)
                p.comp("tensor", t_o, key="td", wait=(("vd", p.tok("vd")),))
                ot = p.tok("td")

                def v_o(v, h=h):
                    v.tensor_scalar_mul(t1[:], pr32[:, 128:256], gcol[:, h:h + 1])
                    return v.tensor_add(otm[:], psol[:, 0:128], t1[:])
                p.comp("vector", v_o, key="vd", wait=(("td", ot),))

                def t_z(t, h=h):
                    return t.transpose(ptr[:, 0:128], zs[:, h * C:(h + 1) * C],
                                       idb[:])
                p.comp("tensor", t_z, key="td", wait=(("vd", p.tok("vd")),))

                def v_gate(v, h=h):
                    return v.tensor_mul(xfg[:, h * 128:(h + 1) * 128], otm[:],
                                        ptr[:, 0:128])
                p.comp("vector", v_gate, key="vd", wait=(("td", p.tok("td")),))
                gatet = p.tok("vd")

                def s_sq(s_, h=h):
                    return s_.activation(sqv[:], xfg[:, h * 128:(h + 1) * 128],
                                         AF.Square)
                p.comp("scalar", s_sq, key="sd", wait=(("vd", gatet),))

                def v_var(v, h=h):
                    return v.reduce_sum(ssq[:, h:h + 1], sqv[:], axis=AX.X)
                p.comp("vector", v_var, key="vd", wait=(("sd", p.tok("sd")),))

                def v_kb(v, g=g, h=h):
                    return v.tensor_scalar_mul(kbt[:], ktm[:, g * 128:(g + 1) * 128],
                                               dccol[:, h:h + 1])
                p.comp("vector", v_kb, key="vd", wait=(("vd", ktt),))
                kbv = p.tok("vd")

                def t_s(t, h=h, xcur=xcur):
                    return t.matmul(psol[:, 128:256], kbt[:],
                                    xsb[:, xcur * 128:(xcur + 1) * 128],
                                    start=True, stop=True)
                p.comp("tensor", t_s, key="td", wait=(("vd", kbv),))
                st_ = p.tok("td")

                def v_s(v, h=h):
                    v.scalar_tensor_tensor(sfp[:, h * 128:(h + 1) * 128],
                                           sfp[:, h * 128:(h + 1) * 128],
                                           gccol[:, h:h + 1],
                                           psol[:, 128:256], ALU.mult, ALU.add)
                    return v.tensor_copy(sbf[:, h * 128:(h + 1) * 128],
                                         sfp[:, h * 128:(h + 1) * 128])
                p.comp("vector", v_s, key="vd", wait=(("td", st_),))

            def v_eps(v):
                return v.tensor_scalar(rscol[:], ssq[:], 1.0 / 128.0, 1e-6,
                                       ALU.mult, ALU.add)
            p.comp("vector", v_eps, key="vd", wait=(("sd", p.tok("sd")),))

            def s_rstd(s_):
                return s_.activation(rscol[:], rscol[:], AF.Sqrt)
            p.comp("scalar", s_rstd, key="sd", wait=(("vd", p.tok("vd")),))

            def v_rstd(v):
                return v.reciprocal(rscol[:], rscol[:])
            p.comp("vector", v_rstd, key="vd", wait=(("sd", p.tok("sd")),))
            rst = p.tok("vd")

            for h in range(NVH):
                def v_xn(v, h=h):
                    return v.scalar_tensor_tensor(
                        xnrm[:, h * 128:(h + 1) * 128],
                        xfg[:, h * 128:(h + 1) * 128],
                        rscol[:, h:h + 1], nwbc[:], ALU.mult, ALU.mult)
                p.comp("vector", v_xn, key="vd", wait=(("vd", rst),))
            xnt = p.tok("vd")

            for grp in range(4):
                def t_tr(t, grp=grp):
                    for i in range(4):
                        h = grp * 4 + i
                        inst = t.transpose(ptr[:, i * 128:(i + 1) * 128],
                                           xnrm[:, h * 128:(h + 1) * 128],
                                           idb[:])
                    return inst
                p.comp("tensor", t_tr, key="td",
                       wait=(("vd", xnt if grp == 0 else p.tok("vd")),))

                def v_tr(v, grp=grp):
                    return v.tensor_copy(stg[:, grp * 512:(grp + 1) * 512],
                                         ptr[:])
                p.comp("vector", v_tr, key="vd",
                       wait=(("td", p.tok("td")), ("st", p.tok("st"))))
            p.dma("scalar", rearr_fp(io["XATT"])[:, :, csl],
                  stg[:].rearrange("q (h t) -> q h t", t=128), key="st",
                  wait=(("vd", p.tok("vd")),))


def emit_moe(p, io):
    nc, es = p.nc, p.es
    E = 16
    TBS = 1024
    xls = es.enter_context(nc.sbuf_tensor("mo_xl", [128, 4 * T], BF16))
    wd = es.enter_context(nc.sbuf_tensor("mo_wd", [16, T], F32))
    accs = es.enter_context(nc.sbuf_tensor("mo_ac", [128, 4 * T], F32))
    w1t = [es.enter_context(nc.sbuf_tensor(f"mo_w1{i}", [128, 4 * 1024], BF16))
           for i in range(2)]
    wgt = [es.enter_context(nc.sbuf_tensor(f"mo_wg{i}", [128, 4 * 1024], BF16))
           for i in range(2)]
    w2t = [es.enter_context(nc.sbuf_tensor(f"mo_w2{i}", [128, 8 * 512], BF16))
           for i in range(2)]
    wbc = es.enter_context(nc.sbuf_tensor("mo_wb", [128, TBS], F32))
    rel = es.enter_context(nc.sbuf_tensor("mo_rl", [128, TBS], F32))
    h1w = es.enter_context(nc.sbuf_tensor("mo_h1", [128, TBS], F32))
    hesb = es.enter_context(nc.sbuf_tensor("mo_he", [128, 8 * TBS], BF16))
    eh = es.enter_context(nc.sbuf_tensor("mo_eh", [16, 16 * 128], F32))
    psg = es.enter_context(nc.psum_tensor("mo_pg", [128, TBS], F32))
    psh = es.enter_context(nc.psum_tensor("mo_ph", [128, TBS], F32))
    psa = es.enter_context(nc.psum_tensor("mo_pa", [128, TBS], F32))
    psw = es.enter_context(nc.psum_tensor("mo_pw", [128, 512], F32))

    p.dma("sync", xls[:].rearrange("q (kc t) -> q kc t", t=T),
          rearr_fp(io["XL"])[:], key="ld")
    p.dma("sync", wd[:], io["WD"][:], key="ld")
    p.dma("sync", eh[:].rearrange("q (h m) -> q h m", m=128),
          io["EH"].rearrange("h q m -> q h m")[:], key="ld")
    ld0 = p.tok("ld")

    wtokm = [0, 0]
    for e in range(E):
        buf = e % 2
        p.dma("sync", w1t[buf][:].rearrange("q (kc m) -> q kc m", m=1024),
              io["RW1T"][e].rearrange("(kc q) m -> q kc m", q=128)[:],
              key="ld", wait=(("td", wtokm[buf]),))
        p.dma("sync", wgt[buf][:].rearrange("q (kc m) -> q kc m", m=1024),
              io["RWGT"][e].rearrange("(kc q) m -> q kc m", q=128)[:],
              key="ld")
        p.dma("sync", w2t[buf][:].rearrange("q (kc m) -> q kc m", m=512),
              io["RW2T"][e].rearrange("(kc q) m -> q kc m", q=128)[:],
              key="ld")
        eld = p.tok("ld")
        for tbi in range(T // TBS):
            t0 = tbi * TBS
            for nb in range(TBS // 512):
                def t_wb(t, e=e, t0=t0, nb=nb):
                    return t.matmul(psw[:], eh[:, e * 128:(e + 1) * 128],
                                    wd[:, t0 + nb * 512:t0 + (nb + 1) * 512],
                                    start=True, stop=True)
                p.comp("tensor", t_wb, key="td",
                       wait=(("ld", eld), ("vd", p.tok("vd"))))

                def v_wb(v, nb=nb):
                    return v.tensor_copy(wbc[:, nb * 512:(nb + 1) * 512],
                                         psw[:])
                p.comp("vector", v_wb, key="vd", wait=(("td", p.tok("td")),))
            wbt = p.tok("vd")

            for m in range(8):
                def t_h(t, buf=buf, m=m, t0=t0):
                    for kc in range(4):
                        for nb in range(TBS // 512):
                            t.matmul(psg[:, nb * 512:(nb + 1) * 512],
                                     wgt[buf][:, kc * 1024 + m * 128:
                                              kc * 1024 + (m + 1) * 128],
                                     xls[:, kc * T + t0 + nb * 512:
                                          kc * T + t0 + (nb + 1) * 512],
                                     start=(kc == 0), stop=(kc == 3))
                    for kc in range(4):
                        for nb in range(TBS // 512):
                            inst = t.matmul(psh[:, nb * 512:(nb + 1) * 512],
                                            w1t[buf][:, kc * 1024 + m * 128:
                                                     kc * 1024 + (m + 1) * 128],
                                            xls[:, kc * T + t0 + nb * 512:
                                                 kc * T + t0 + (nb + 1) * 512],
                                            start=(kc == 0), stop=(kc == 3))
                    return inst
                p.comp("tensor", t_h, key="td",
                       wait=(("vd", p.tok("vd")), ("sd", p.tok("sd"))))
                mmt = p.tok("td")

                def s_rel(s_):
                    return s_.activation(rel[:], psg[:], AF.Relu)
                p.comp("scalar", s_rel, key="sd", wait=(("td", mmt),))

                def v_he(v, m=m):
                    v.tensor_mul(h1w[:], psh[:], wbc[:])
                    v.tensor_mul(rel[:], rel[:], rel[:])
                    return v.tensor_mul(hesb[:, m * TBS:(m + 1) * TBS],
                                        rel[:], h1w[:])
                p.comp("vector", v_he, key="vd", wait=(("sd", p.tok("sd")),))
            het = p.tok("vd")

            for m2 in range(4):
                def t_a(t, buf=buf, m2=m2):
                    for kc in range(8):
                        for nb in range(TBS // 512):
                            inst = t.matmul(psa[:, nb * 512:(nb + 1) * 512],
                                            w2t[buf][:, kc * 512 + m2 * 128:
                                                     kc * 512 + (m2 + 1) * 128],
                                            hesb[:, kc * TBS + nb * 512:
                                                 kc * TBS + (nb + 1) * 512],
                                            start=(kc == 0), stop=(kc == 7))
                    return inst
                p.comp("tensor", t_a, key="td",
                       wait=(("vd", p.tok("vd")), ("sd", p.tok("sd"))))
                at = p.tok("td")

                def v_a(v, e=e, m2=m2, t0=t0):
                    dst = accs[:, m2 * T + t0: m2 * T + t0 + TBS]
                    if e == 0:
                        return v.tensor_copy(dst, psa[:])
                    return v.tensor_add(dst, dst, psa[:])
                p.comp("vector", v_a, key="vd", wait=(("td", at),))
        wtokm[buf] = p.tok("td")

    accv = rearr_fp(io["ACC"])
    for kc in range(4):
        def v_cast(v, kc=kc):
            return v.tensor_copy(hesb[:, 0:T], accs[:, kc * T:(kc + 1) * T])
        p.comp("vector", v_cast, key="vd", wait=(("st", p.tok("st")),))
        p.dma("scalar", accv[:, kc, :], hesb[:, 0:T], key="st",
              wait=(("vd", p.tok("vd")),))


MODULE_INPUTS = {
    "L0_attn_Wq": ([2048, 2048], "f32"), "L0_attn_Wk": ([2048, 512], "f32"),
    "L0_attn_Wv": ([2048, 512], "f32"), "L0_attn_Wo": ([2048, 2048], "f32"),
    "L0_ffn_w1": ([5632, 2048], "f32"), "L0_ffn_w2": ([5632, 2048], "f32"),
    "L0_ffn_w3": ([2048, 5632], "f32"),
    "L1_attn_w_qkvz": ([6144, 2048], "f32"), "L1_attn_w_ba": ([32, 2048], "f32"),
    "L1_attn_conv_w": ([4096, 4], "f32"), "L1_attn_dt_bias": ([16, 1], "f32"),
    "L1_attn_A_log": ([16, 1], "f32"), "L1_attn_norm_weight": ([1, 128], "f32"),
    "L1_attn_w_out": ([2048, 2048], "f32"),
    "L1_ffn_sw1": ([1024, 2048], "f32"), "L1_ffn_swg": ([1024, 2048], "f32"),
    "L1_ffn_sw2": ([2048, 1024], "f32"),
    "L1_ffn_rw1": ([16, 1024, 512], "f32"), "L1_ffn_rwg": ([16, 1024, 512], "f32"),
    "L1_ffn_rw2": ([16, 512, 1024], "f32"),
    "L1_ffn_Wg": ([16, 2048], "f32"), "L1_ffn_Wdown": ([512, 2048], "f32"),
    "L1_ffn_Wup": ([2048, 512], "f32"), "L1_ffn_biases": ([1, 16], "f32"),
    "X0": ([B, SEQ, D], "f32"),
}

SCRATCH = {
    "WQB": ([2048, 2048], "bf16"), "WKB": ([2048, 512], "bf16"),
    "WVB": ([2048, 512], "bf16"), "WOB": ([2048, 2048], "bf16"),
    "WQKVZT": ([2048, 6144], "bf16"), "WBAT": ([2048, 32], "bf16"),
    "WOUTT": ([2048, 2048], "bf16"), "SW1T": ([2048, 1024], "bf16"),
    "SWGT": ([2048, 1024], "bf16"), "SW2T": ([1024, 2048], "bf16"),
    "WDOWNT": ([2048, 512], "bf16"), "WGT": ([2048, 16], "bf16"),
    "WUPT": ([512, 2048], "bf16"),
    "RW1T": ([16, 512, 1024], "bf16"), "RWGT": ([16, 512, 1024], "bf16"),
    "RW2T": ([16, 1024, 512], "bf16"),
    "Q1T": ([2048, 5632], "bf16"), "Q2T": ([2048, 5632], "bf16"),
    "Q3T": ([5632, 2048], "bf16"),
    "XF0": ([2048, T], "f32"), "XN0": ([2048, T], "bf16"),
    "QF": ([2048, T], "bf16"), "KF": ([512, T], "bf16"), "VF": ([512, T], "bf16"),
    "OA": ([2048, T], "bf16"), "HID": ([5632, T], "bf16"),
    "XF1": ([2048, T], "f32"), "XN1": ([2048, T], "bf16"),
    "QKVZ": ([6144, T], "bf16"), "BAT": ([32, T], "f32"),
    "QN": ([1024, T], "bf16"), "KN": ([1024, T], "bf16"),
    "VC": ([2048, T], "bf16"), "ZS": ([2048, T], "bf16"),
    "CBE": ([32, T], "f32"), "XATT": ([2048, T], "bf16"),
    "HIDS": ([1024, T], "bf16"), "XL": ([512, T], "bf16"),
    "WD": ([16, T], "f32"), "ACC": ([512, T], "bf16"),
    "X2F": ([2048, T], "f32"),
    "OUT": ([B, SEQ, D], "f32"),
}


def spec_of(name):
    for tbl in (MODULE_INPUTS, CONST_SPECS, SCRATCH):
        if name in tbl:
            return tbl[name]
    raise KeyError(name)


PHASES = [
    ("prep_w", emit_prep_w,
     ["L0_attn_Wq", "L0_attn_Wk", "L0_attn_Wv", "L0_attn_Wo",
      "L1_attn_w_qkvz", "L1_attn_w_ba", "L1_attn_w_out",
      "L1_ffn_sw1", "L1_ffn_swg", "L1_ffn_sw2", "L1_ffn_Wdown", "L1_ffn_Wg",
      "L1_ffn_Wup", "L1_ffn_rw1", "L1_ffn_rwg", "L1_ffn_rw2", "IDBF"],
     ["WQB", "WKB", "WVB", "WOB", "WQKVZT", "WBAT", "WOUTT", "SW1T", "SWGT",
      "SW2T", "WDOWNT", "WGT", "WUPT", "RW1T", "RWGT", "RW2T"]),
    ("bitq", emit_bitq,
     ["L0_ffn_w1", "L0_ffn_w2", "L0_ffn_w3", "IDBF", "ONES128", "ONES1"],
     ["Q1T", "Q2T", "Q3T"]),
    ("xin", emit_xin, ["X0", "IDF32"], ["XF0"]),
    ("ln0", lambda p, io: emit_ln(p, io, "XF0", "XN0", tag="ln0"),
     ["XF0", "ONES128", "ONES1"], ["XN0"]),
    ("proj0", emit_proj0, ["XN0", "WQB", "WKB", "WVB"], ["QF", "KF", "VF"]),
    ("attn0", emit_attn0, ["QF", "KF", "VF", "CMASK", "IDBF"], ["OA"]),
    ("ffn12", emit_ffn12, ["XN0", "Q1T", "Q2T"], ["HID"]),
    ("asm0", emit_asm0, ["HID", "OA", "Q3T", "WOB", "XF0"], ["XF1"]),
    ("ln1", lambda p, io: emit_ln(p, io, "XF1", "XN1", tag="ln1"),
     ["XF1", "ONES128", "ONES1"], ["XN1"]),
    ("proj1", emit_proj1, ["XN1", "WQKVZT", "WBAT"], ["QKVZ", "BAT"]),
    ("conv", emit_conv,
     ["QKVZ", "BAT", "L1_attn_conv_w", "L1_attn_A_log", "L1_attn_dt_bias",
      "ONES128", "ONES1"],
     ["QN", "KN", "VC", "ZS", "CBE"]),
    ("scan", emit_scan,
     ["QN", "KN", "VC", "ZS", "CBE", "L1_attn_norm_weight", "MASKS_POS",
      "MASKU_INC", "EH", "IDF32", "IDBF", "ONES1"],
     ["XATT"]),
    ("shared", emit_shared, ["XN1", "SW1T", "SWGT"], ["HIDS"]),
    ("gatex", emit_gatex, ["XN1", "WDOWNT"], ["XL"]),
    ("gate", emit_gate, ["XN1", "WGT", "L1_ffn_biases", "IDF32", "ONES1"],
     ["WD"]),
    ("moe", emit_moe, ["XL", "WD", "RW1T", "RWGT", "RW2T", "EH"], ["ACC"]),
    ("asm1", emit_asm1,
     ["XATT", "WOUTT", "HIDS", "SW2T", "ACC", "WUPT", "XF1"], ["X2F"]),
    ("xout", lambda p, io: emit_xout(p, io), ["X2F", "IDF32"], ["OUT"]),
]


def build_phase(name, emit, ins, outs):
    nc = bass.Bass("TRN2", target_bir_lowering=False)
    es = ExitStack()
    with es:
        aps = {}
        for n in ins:
            shape, dt = spec_of(n)
            aps[n] = nc.dram_tensor(n, shape, DT[dt], kind="ExternalInput").ap()
        for n in outs:
            shape, dt = spec_of(n)
            aps[n] = nc.dram_tensor(n, shape, DT[dt], kind="ExternalOutput").ap()
        p = Prog(nc, es, name)
        emit(p, aps)
        p.finalize()
        block = es.enter_context(nc.Block(name))
        p.emit(block)
    return nc


_PHASE_CACHE = {}


def run_phase(name, env):
    for pname, emit, ins, outs in PHASES:
        if pname == name:
            break
    else:
        raise KeyError(name)
    if name not in _PHASE_CACHE:
        _PHASE_CACHE[name] = build_phase(name, emit, ins, outs)
    nc = _PHASE_CACHE[name]
    in_maps = [{n: env[c][n] for n in ins} for c in range(len(env))]
    res = run_bass_kernel_spmd(nc, in_maps, core_ids=list(range(len(env))))
    for c in range(len(env)):
        for n in outs:
            env[c][n] = np.asarray(res.results[c][n])
    return env


def kernel(L0_attn_x, **weights):
    inputs = dict(weights)
    inputs["L0_attn_x"] = L0_attn_x
    consts = host_consts()
    ncores = 8
    env = []
    shaped = {}
    for k in MODULE_INPUTS:
        if k == "X0":
            continue
        shape, dt = MODULE_INPUTS[k]
        shaped[k] = np.ascontiguousarray(
            np.asarray(inputs[k], NPDT[dt]).reshape(shape))
    for c in range(ncores):
        e = dict(shaped)
        e.update(consts)
        e["X0"] = np.ascontiguousarray(
            np.asarray(L0_attn_x, np.float32)[B * c: B * (c + 1)])
        env.append(e)
    for pname, _, _, _ in PHASES:
        env = run_phase(pname, env)
    out = np.concatenate([env[c]["OUT"] for c in range(ncores)], axis=0)
    return np.ascontiguousarray(out.astype(np.float32))


_FUSED_CACHE = []


def build_fused():
    nc = bass.Bass("TRN2", target_bir_lowering=False)
    outer = ExitStack()
    with outer:
        aps = {}
        for n, (shape, dt) in {**MODULE_INPUTS, **CONST_SPECS}.items():
            aps[n] = nc.dram_tensor(n, shape, DT[dt], kind="ExternalInput").ap()
        for n, (shape, dt) in SCRATCH.items():
            kind = "ExternalOutput" if n == "OUT" else "Internal"
            aps[n] = nc.dram_tensor(n, shape, DT[dt], kind=kind).ap()
        p = Prog(nc, outer, "mk", sem_es=outer)
        for name, emit, ins, outs in PHASES:
            with ExitStack() as phase_es:
                p.es = phase_es
                emit(p, aps)
            p.barrier()
        p.finalize()
        block = outer.enter_context(nc.Block("mk"))
        p.emit(block)
    return nc


def kernel_fused(L0_attn_x, **weights):
    inputs = dict(weights)
    consts = host_consts()
    ncores = 8
    shaped = {}
    for k in MODULE_INPUTS:
        if k == "X0":
            continue
        shape, dt = MODULE_INPUTS[k]
        shaped[k] = np.ascontiguousarray(
            np.asarray(inputs[k], NPDT[dt]).reshape(shape))
    shaped.update(consts)
    if not _FUSED_CACHE:
        _FUSED_CACHE.append(build_fused())
    nc = _FUSED_CACHE[0]
    x = np.asarray(L0_attn_x, np.float32)
    in_maps = []
    for c in range(ncores):
        e = dict(shaped)
        e["X0"] = np.ascontiguousarray(x[B * c: B * (c + 1)])
        in_maps.append(e)
    res = run_bass_kernel_spmd(nc, in_maps, core_ids=list(range(ncores)))
    out = np.concatenate([np.asarray(res.results[c]["OUT"])
                          for c in range(ncores)], axis=0)
    return np.ascontiguousarray(out.astype(np.float32))


_ORIG_KERNEL_MULTI = kernel


def kernel(L0_attn_x, **weights):
    return kernel_fused(L0_attn_x, **weights)


# version 04 (final)
# speedup vs baseline: 1.1818x; 1.1818x over previous
import numpy as np
import ml_dtypes
from contextlib import ExitStack

import concourse.bass as bass
import concourse.mybir as mybir
from concourse.bass_utils import run_bass_kernel_spmd

F32 = mybir.dt.float32
BF16 = mybir.dt.bfloat16
AF = mybir.ActivationFunctionType
ALU = mybir.AluOpType
AX = mybir.AxisListType
BF = ml_dtypes.bfloat16

B = 2
SEQ = 2048
T = B * SEQ
D = 2048
NEG = -60000.0
BIG = 60000.0

DT = {"f32": F32, "bf16": BF16}
NPDT = {"f32": np.float32, "bf16": BF}


class Prog:
    ENGINES = ("sync", "tensor", "vector", "scalar", "gpsimd")

    def __init__(self, nc, es, name, sem_es=None):
        self.nc = nc
        self.es = es
        self.sem_es = sem_es if sem_es is not None else es
        self.name = name
        self.steps = {e: [] for e in self.ENGINES}
        self.sem = {}
        self.cnt = {}

    def _s(self, key):
        if key not in self.sem:
            self.sem[key] = self.sem_es.enter_context(
                self.nc.semaphore(f"{self.name}_{key}"))
            self.cnt[key] = 0
        return self.sem[key]

    def barrier(self):
        waits = tuple((k, self.cnt[k]) for k in self.cnt if self.cnt[k] > 0)
        if not waits:
            return
        for ename in self.ENGINES:
            self.step(ename, lambda e: None, wait=waits)

    def tok(self, key):
        self._s(key)
        return self.cnt[key]

    def step(self, eng, fn, wait=(), inc=()):
        waits = [(self._s(k), v) for (k, v) in wait]
        incs = []
        for k, amt in inc:
            self._s(k)
            self.cnt[k] += amt
            incs.append((self.sem[k], amt))
        self.steps[eng].append((waits, fn, incs))
        return {k: self.cnt[k] for (k, _) in inc}

    def dma(self, eng, out, in_, key="ld", wait=()):
        def fn(e):
            return e.dma_start(out=out, in_=in_)
        self.step(eng, fn, wait=wait, inc=((key, 16),))
        return self.cnt[key]

    def comp(self, eng, fn, key=None, wait=()):
        inc = ((key, 1),) if key else ()
        self.step(eng, fn, wait=wait, inc=inc)
        return self.cnt[key] if key else None

    def finalize(self):
        waits = tuple((k, self.cnt[k]) for k in ("ld", "st")
                      if k in self.cnt and self.cnt[k] > 0)
        if not waits:
            return
        for ename in self.ENGINES:
            if self.steps[ename]:
                self.step(ename, lambda e: None, wait=waits)
                return

    OWN = {"vector": "vd", "scalar": "sd", "gpsimd": "pd", "tensor": "td"}

    def emit(self, block):
        for ename in self.ENGINES:
            steps = self.steps[ename]
            if not steps:
                continue
            own = self.OWN.get(ename)

            def run(eng, steps=steps, own=own):
                last_own = 0
                for waits, fn, incs in steps:
                    if own is not None and last_own > 0:
                        eng.wait_ge(self.sem[own], last_own)
                    for sem, val in waits:
                        eng.wait_ge(sem, val)
                    inst = fn(eng)
                    for sem, amt in incs:
                        assert inst is not None, "inc on step returning None"
                        inst = inst.then_inc(sem, amt)
                        if own is not None and sem is self.sem.get(own):
                            last_own += amt
            getattr(block, ename)(run)


def rearr_fp(dram_ap, p=128):
    return dram_ap.rearrange("(fc p) t -> p fc t", p=p)


def host_consts():
    c = {}
    c["IDF32"] = np.eye(128, dtype=np.float32)
    c["IDBF"] = np.eye(128, dtype=BF)
    c["ONES128"] = np.ones((128, 1), np.float32)
    c["ONES1"] = np.ones((1, 128), np.float32)
    cm = np.zeros((4, 128, 512), np.float32)
    for r in range(4):
        q = np.arange(128)[:, None]
        k = np.arange(512)[None, :]
        cm[r] = np.where(k <= r * 128 + q, 0.0, NEG)
    c["CMASK"] = cm
    tau = np.arange(128)[:, None]
    j = np.arange(128)[None, :]
    c["MASKS_POS"] = np.where(j >= tau, BIG, 0.0).astype(np.float32)
    c["MASKU_INC"] = np.where(j >= tau, 0.0, NEG).astype(np.float32)
    eh = np.zeros((16, 16, 128), np.float32)
    for h in range(16):
        eh[h, h, :] = 1.0
    c["EH"] = eh
    return c


CONST_SPECS = {
    "IDF32": ([128, 128], "f32"),
    "IDBF": ([128, 128], "bf16"),
    "ONES128": ([128, 1], "f32"),
    "ONES1": ([1, 128], "f32"),
    "CMASK": ([4, 128, 512], "f32"),
    "MASKS_POS": ([128, 128], "f32"),
    "MASKU_INC": ([128, 128], "f32"),
    "EH": ([16, 16, 128], "f32"),
}


PREP_CASTS = [
    ("L0_attn_Wq", "WQB", [2048, 2048]),
    ("L0_attn_Wk", "WKB", [2048, 512]),
    ("L0_attn_Wv", "WVB", [2048, 512]),
    ("L0_attn_Wo", "WOB", [2048, 2048]),
]
PREP_TRANS = [
    ("L1_attn_w_qkvz", "WQKVZT", 6144, 2048),
    ("L1_attn_w_ba", "WBAT", 32, 2048),
    ("L1_attn_w_out", "WOUTT", 2048, 2048),
    ("L1_ffn_sw1", "SW1T", 1024, 2048),
    ("L1_ffn_swg", "SWGT", 1024, 2048),
    ("L1_ffn_sw2", "SW2T", 2048, 1024),
    ("L1_ffn_Wdown", "WDOWNT", 512, 2048),
    ("L1_ffn_Wg", "WGT", 16, 2048),
    ("L1_ffn_Wup", "WUPT", 2048, 512),
]


def emit_prep_w(p, io):
    nc = p.nc
    es = p.es
    for src, dst, _shape in PREP_CASTS:
        p.dma("gpsimd", io[dst][:], io[src][:], key="st")

    src_sb = [es.enter_context(nc.sbuf_tensor(f"pw_src{i}", [128, 2048], BF16))
              for i in range(2)]
    stg_sb = [es.enter_context(nc.sbuf_tensor(f"pw_stg{i}", [128, 2048 * 2], BF16))
              for i in range(2)]
    ps = es.enter_context(nc.psum_tensor("pw_ps", [128, 1024], BF16))
    idb = es.enter_context(nc.sbuf_tensor("pw_idb", [128, 128], BF16))
    p.dma("gpsimd", idb[:], io["IDBF"][:])
    ld0 = p.tok("ld")

    jobs = []
    for src, dst, M, K in PREP_TRANS:
        jobs.append((io[src], io[dst], M, K))
    for e in range(16):
        jobs.append((io["L1_ffn_rw1"][e], io["RW1T"][e], 1024, 512))
        jobs.append((io["L1_ffn_rwg"][e], io["RWGT"][e], 1024, 512))
        jobs.append((io["L1_ffn_rw2"][e], io["RW2T"][e], 512, 1024))

    it = 0
    for src, dst, M, K in jobs:
        nk = K // 128
        dstv = dst.rearrange("(kc q) m -> q kc m", q=128)
        for mb in range((M + 127) // 128):
            rows = min(128, M - mb * 128)
            buf = it % 2
            p.dma("gpsimd", src_sb[buf][0:rows, 0:K],
                  src[mb * 128: mb * 128 + rows, :],
                  key="ld", wait=(("td", p.tok("td")),))
            ldt = p.tok("ld")
            for g0 in range(0, nk, 8):
                gn = min(8, nk - g0)

                def tfn(t, buf=buf, g0=g0, gn=gn, rows=rows):
                    for i in range(gn):
                        inst = t.transpose(
                            ps[0:128, i * 128: i * 128 + rows],
                            src_sb[buf][0:rows, (g0 + i) * 128:(g0 + i + 1) * 128],
                            idb[0:rows, 0:rows])
                    return inst
                p.comp("tensor", tfn, key="td",
                       wait=(("ld", ldt), ("vd", p.tok("vd"))))
                tdt = p.tok("td")

                def vfn(v, buf=buf, g0=g0, gn=gn, rows=rows):
                    if rows != 128:
                        return v.tensor_copy(
                            stg_sb[buf][0:128, g0 * 128:(g0 + gn) * 128]
                            .rearrange("q (kc m) -> q kc m", m=128)[:, :, 0:rows],
                            ps[0:128, 0:gn * 128]
                            .rearrange("q (kc m) -> q kc m", m=128)[:, :, 0:rows])
                    return v.tensor_copy(
                        stg_sb[buf][0:128, g0 * 128:(g0 + gn) * 128],
                        ps[0:128, 0:gn * 128])
                p.comp("vector", vfn, key="vd",
                       wait=(("td", tdt), ("st", p.tok("st"))))
            p.dma("scalar",
                  dstv[:, :, mb * 128: mb * 128 + rows],
                  stg_sb[buf][0:128, 0:nk * 128]
                  .rearrange("q (kc m) -> q kc m", m=128)[:, :, 0:rows],
                  key="st", wait=(("vd", p.tok("vd")),))
            it += 1


def emit_bitq(p, io):
    nc, es = p.nc, p.es
    w_specs = [
        ("L0_ffn_w1", "Q1T", 5632, 2048),
        ("L0_ffn_w2", "Q2T", 5632, 2048),
        ("L0_ffn_w3", "Q3T", 2048, 5632),
    ]
    src32 = [es.enter_context(nc.sbuf_tensor(f"bq_s{i}", [128, 5632], F32))
             for i in range(2)]
    acc = es.enter_context(nc.sbuf_tensor("bq_acc", [128, 8], F32))
    part = es.enter_context(nc.sbuf_tensor("bq_part", [128, 8], F32))
    ones = es.enter_context(nc.sbuf_tensor("bq_ones", [128, 1], F32))
    ones1 = es.enter_context(nc.sbuf_tensor("bq_ones1", [1, 128], F32))
    throw = es.enter_context(nc.sbuf_tensor("bq_throw", [1, 4], F32))
    thcol = es.enter_context(nc.sbuf_tensor("bq_thcol", [128, 8], F32))
    pstat = es.enter_context(nc.psum_tensor("bq_pstat", [128, 8], F32))
    qt = [es.enter_context(nc.sbuf_tensor(f"bq_q{i}", [128, 5632], BF16))
          for i in range(2)]
    stg = [es.enter_context(nc.sbuf_tensor(f"bq_stg{i}", [128, 5632], BF16))
           for i in range(2)]
    psq = es.enter_context(nc.psum_tensor("bq_psq", [128, 1024], BF16))
    idb = es.enter_context(nc.sbuf_tensor("bq_idb", [128, 128], BF16))
    p.dma("gpsimd", idb[:], io["IDBF"][:])
    p.dma("sync", ones[:], io["ONES128"][:])
    p.dma("sync", ones1[:], io["ONES1"][:])

    it = 0
    for wi, (src_name, _, M, K) in enumerate(w_specs):
        src = io[src_name]
        nmb = M // 128
        for mb in range(nmb):
            buf = it % 2
            p.dma("sync", src32[buf][:, 0:K], src[mb * 128:(mb + 1) * 128, :],
                  key="ld", wait=(("vd", p.tok("vd")),))
            ldt = p.tok("ld")

            def vred(v, buf=buf, wi=wi, K=K):
                return v.reduce_sum(part[:, wi:wi + 1], src32[buf][:, 0:K],
                                    axis=AX.X, apply_absolute_value=True)
            p.comp("vector", vred, key="vd", wait=(("ld", ldt),))

            def vacc(v, wi=wi, mb=mb):
                if mb == 0:
                    return v.tensor_copy(acc[:, wi:wi + 1], part[:, wi:wi + 1])
                return v.tensor_add(acc[:, wi:wi + 1], acc[:, wi:wi + 1],
                                    part[:, wi:wi + 1])
            p.comp("vector", vacc, key="vd")
            it += 1
    vd_all = p.tok("vd")

    def tstat(t):
        return t.matmul(pstat[0:1, 0:4], ones[:], acc[:, 0:4],
                        start=True, stop=True)
    p.comp("tensor", tstat, key="td", wait=(("vd", vd_all),))

    def vth(v):
        n = 5632.0 * 2048.0
        return v.tensor_scalar(throw[0:1, 0:4], pstat[0:1, 0:4],
                               0.5 / n, 0.5e-8, ALU.mult, ALU.add)
    p.comp("vector", vth, key="vd", wait=(("td", p.tok("td")),))

    def tbc(t):
        return t.matmul(pstat[0:128, 4:8], ones1[:], throw[0:1, 0:4],
                        start=True, stop=True)
    p.comp("tensor", tbc, key="td", wait=(("vd", p.tok("vd")),))

    def vbc(v):
        v.tensor_copy(thcol[:, 0:4], pstat[:, 4:8])
        return v.tensor_scalar_mul(thcol[:, 4:8], pstat[:, 4:8], -1.0)
    p.comp("vector", vbc, key="vd", wait=(("td", p.tok("td")),))
    prep_vd = p.tok("vd")

    it = 0
    for wi, (src_name, dst_name, M, K) in enumerate(w_specs):
        src, dst = io[src_name], io[dst_name]
        dstv = dst.rearrange("(kc q) m -> q kc m", q=128)
        nk = K // 128
        for mb in range(M // 128):
            buf = it % 2
            p.dma("sync", src32[buf][:, 0:K], src[mb * 128:(mb + 1) * 128, :],
                  key="ld", wait=(("vd", max(prep_vd, p.tok("vd"))),))
            ldt = p.tok("ld")

            def vq(v, buf=buf, wi=wi, K=K):
                v.tensor_single_scalar(qt[buf][:, 0:K], src32[buf][:, 0:K],
                                       thcol[:, wi:wi + 1], ALU.is_ge)
                v.tensor_single_scalar(src32[buf][:, 0:K], src32[buf][:, 0:K],
                                       thcol[:, 4 + wi:5 + wi], ALU.is_le)
                return v.tensor_sub(qt[buf][:, 0:K], qt[buf][:, 0:K],
                                    src32[buf][:, 0:K])
            p.comp("vector", vq, key="vd", wait=(("ld", ldt),))
            qvd = p.tok("vd")
            for g0 in range(0, nk, 8):
                gn = min(8, nk - g0)

                def tfn(t, buf=buf, g0=g0, gn=gn):
                    for i in range(gn):
                        inst = t.transpose(
                            psq[:, i * 128:(i + 1) * 128],
                            qt[buf][:, (g0 + i) * 128:(g0 + i + 1) * 128],
                            idb[:])
                    return inst
                p.comp("tensor", tfn, key="td",
                       wait=(("vd", qvd if g0 == 0 else p.tok("vd")),))

                def vev(v, buf=buf, g0=g0, gn=gn):
                    return v.tensor_copy(stg[buf][:, g0 * 128:(g0 + gn) * 128],
                                         psq[:, 0:gn * 128])
                p.comp("vector", vev, key="vd", wait=(("td", p.tok("td")),))
            p.dma("scalar", dstv[:, :, mb * 128:(mb + 1) * 128],
                  stg[buf][:, 0:nk * 128].rearrange("q (kc m) -> q kc m", m=128),
                  key="st", wait=(("vd", p.tok("vd")),))
            it += 1


def emit_xin(p, io):
    nc, es = p.nc, p.es
    x0 = io["X0"]
    xf = io["XF0"]
    xfv = rearr_fp(xf)
    src = [es.enter_context(nc.sbuf_tensor(f"xi_s{i}", [128, 2048], F32))
           for i in range(2)]
    stg = [es.enter_context(nc.sbuf_tensor(f"xi_g{i}", [128, 2048], F32))
           for i in range(2)]
    ps = es.enter_context(nc.psum_tensor("xi_ps", [128, 2048], F32))
    idf = es.enter_context(nc.sbuf_tensor("xi_id", [128, 128], F32))
    p.dma("sync", idf[:], io["IDF32"][:])

    it = 0
    for s in range(B):
        for ti in range(SEQ // 128):
            buf = it % 2
            p.dma("sync", src[buf][:],
                  x0[s, ti * 128:(ti + 1) * 128, :], key="ld",
                  wait=(("td", p.tok("td")),))
            ldt = p.tok("ld")
            evd = p.tok("vd")
            for g in range(4):
                def tfn(t, buf=buf, g=g):
                    for i in range(4):
                        fc = g * 4 + i
                        inst = t.transpose(ps[:, g * 512 + i * 128: g * 512 + (i + 1) * 128],
                                           src[buf][:, fc * 128:(fc + 1) * 128],
                                           idf[:])
                    return inst
                p.comp("tensor", tfn, key="td",
                       wait=(("ld", ldt), ("vd", evd)))

                def vfn(v, buf=buf, g=g):
                    return v.tensor_copy(stg[buf][:, g * 512:(g + 1) * 512],
                                         ps[:, g * 512:(g + 1) * 512])
                p.comp("vector", vfn, key="vd", wait=(("td", p.tok("td")),))
            tcol = s * SEQ + ti * 128
            p.dma("scalar", xfv[:, :, tcol:tcol + 128],
                  stg[buf][:].rearrange("q (fc t) -> q fc t", t=128),
                  key="st", wait=(("vd", p.tok("vd")),))
            it += 1


def emit_ln(p, io, src_name, dst_name, tag="ln"):
    nc, es = p.nc, p.es
    xfv = rearr_fp(io[src_name])
    xnv = rearr_fp(io[dst_name])
    xblk = [es.enter_context(nc.sbuf_tensor(f"{tag}_x{i}", [128, 16 * 512], F32))
            for i in range(2)]
    sq = es.enter_context(nc.sbuf_tensor(f"{tag}_sq", [128, 16 * 512], F32))
    xnb = [es.enter_context(nc.sbuf_tensor(f"{tag}_o{i}", [128, 16 * 512], BF16))
           for i in range(2)]
    tmp = es.enter_context(nc.sbuf_tensor(f"{tag}_tmp", [128, 512], F32))
    rowbuf = es.enter_context(nc.sbuf_tensor(f"{tag}_row", [1, 512 * 4], F32))
    ones = es.enter_context(nc.sbuf_tensor(f"{tag}_ones", [128, 1], F32))
    ones1 = es.enter_context(nc.sbuf_tensor(f"{tag}_ones1", [1, 128], F32))
    ps_s = es.enter_context(nc.psum_tensor(f"{tag}_ps", [1, 1024], F32))
    ps_bc = es.enter_context(nc.psum_tensor(f"{tag}_bc", [128, 1024], F32))
    p.dma("sync", ones[:], io["ONES128"][:])
    p.dma("sync", ones1[:], io["ONES1"][:])

    MU, RS = 0, 1
    for tbi in range(T // 512):
        buf = tbi % 2
        tsl = slice(tbi * 512, (tbi + 1) * 512)
        p.dma("sync", xblk[buf][:].rearrange("q (fc t) -> q fc t", t=512),
              xfv[:, :, tsl], key="ld",
              wait=(("vd", p.tok("vd")), ("td", p.tok("td"))))
        ldt = p.tok("ld")

        def sfn(s, buf=buf):
            return s.activation(sq[:], xblk[buf][:], AF.Square)
        p.comp("scalar", sfn, key="sd", wait=(("ld", ldt), ("td", p.tok("td"))))
        sdt = p.tok("sd")

        def tfn(t, buf=buf):
            for fc in range(16):
                t.matmul(ps_s[0:1, 0:512], ones[:],
                         xblk[buf][:, fc * 512:(fc + 1) * 512],
                         start=(fc == 0), stop=(fc == 15))
            for fc in range(16):
                inst = t.matmul(ps_s[0:1, 512:1024], ones[:],
                                sq[:, fc * 512:(fc + 1) * 512],
                                start=(fc == 0), stop=(fc == 15))
            return inst
        p.comp("tensor", tfn, key="td",
               wait=(("ld", ldt), ("sd", sdt), ("vd", p.tok("vd"))))
        tdt = p.tok("td")

        def vstat(v):
            v.tensor_scalar_mul(rowbuf[0:1, 0:512], ps_s[0:1, 0:512], 1.0 / D)
            v.tensor_scalar_mul(rowbuf[0:1, 512:1024], ps_s[0:1, 512:1024], 1.0 / D)
            v.tensor_mul(rowbuf[0:1, 1024:1536], rowbuf[0:1, 0:512], rowbuf[0:1, 0:512])
            v.tensor_sub(rowbuf[0:1, 512:1024], rowbuf[0:1, 512:1024],
                         rowbuf[0:1, 1024:1536])
            return v.tensor_scalar_add(rowbuf[0:1, 512:1024],
                                       rowbuf[0:1, 512:1024], 1e-5)
        p.comp("vector", vstat, key="vd", wait=(("td", tdt),))

        def s_sqrt(s):
            return s.activation(rowbuf[0:1, 1024:1536], rowbuf[0:1, 512:1024],
                                AF.Sqrt)
        p.comp("scalar", s_sqrt, key="sd", wait=(("vd", p.tok("vd")),))

        def v_rec(v):
            return v.reciprocal(rowbuf[0:1, 1536:2048], rowbuf[0:1, 1024:1536])
        p.comp("vector", v_rec, key="vd", wait=(("sd", p.tok("sd")),))

        def t_bc(t):
            t.matmul(ps_bc[:, 0:512], ones1[:], rowbuf[0:1, 0:512],
                     start=True, stop=True)
            return t.matmul(ps_bc[:, 512:1024], ones1[:], rowbuf[0:1, 1536:2048],
                            start=True, stop=True)
        p.comp("tensor", t_bc, key="td", wait=(("vd", p.tok("vd")),))
        bct = p.tok("td")

        def v_apply(v, buf=buf):
            for fc in range(16):
                v.tensor_sub(tmp[:], xblk[buf][:, fc * 512:(fc + 1) * 512],
                             ps_bc[:, 0:512])
                inst = v.tensor_mul(xnb[buf][:, fc * 512:(fc + 1) * 512],
                                    tmp[:], ps_bc[:, 512:1024])
            return inst
        p.comp("vector", v_apply, key="vd",
               wait=(("td", bct), ("st", p.tok("st"))))
        p.dma("scalar", xnv[:, :, tsl],
              xnb[buf][:].rearrange("q (fc t) -> q fc t", t=512),
              key="st", wait=(("vd", p.tok("vd")),))


def emit_sweep(p, io, specs, tb, tag):
    nc, es = p.nc, p.es
    xsrcs = {}
    for sp in specs:
        for lists in (sp["klist"], sp.get("klist2") or []):
            for wt, xn, K in lists:
                xsrcs[xn] = K // 128
    xblk = {xn: es.enter_context(
        nc.sbuf_tensor(f"{tag}_x_{xn}", [128, nkc * tb], BF16))
        for xn, nkc in xsrcs.items()}
    maxslots = max(sum(K // 128 for _, _, K in sp["klist"]) +
                   sum(K // 128 for _, _, K in (sp.get("klist2") or []))
                   for sp in specs)
    wtile = [es.enter_context(
        nc.sbuf_tensor(f"{tag}_w{i}", [128, maxslots * 128], BF16))
        for i in range(2)]
    stage = [es.enter_context(nc.sbuf_tensor(f"{tag}_o{i}", [128, tb], F32))
             for i in range(2)]
    stageb = [es.enter_context(nc.sbuf_tensor(f"{tag}_ob{i}", [128, tb], BF16))
              for i in range(2)]
    aux_sb = [es.enter_context(nc.sbuf_tensor(f"{tag}_a{i}", [128, tb], F32))
              for i in range(2)]
    tmp1 = es.enter_context(nc.sbuf_tensor(f"{tag}_t1", [128, tb], F32))
    ps = [es.enter_context(nc.psum_tensor(f"{tag}_p{i}", [128, tb], F32))
          for i in range(2)]
    paired = any(sp.get("klist2") for sp in specs)
    ps2 = [es.enter_context(nc.psum_tensor(f"{tag}_q{i}", [128, tb], F32))
           for i in range(2)] if paired else None

    it = 0
    wtok = [0, 0]
    atok = [0, 0]
    sdtok = [0, 0]
    stok = [0, 0]
    for tbi in range(T // tb):
        tsl = slice(tbi * tb, (tbi + 1) * tb)
        for xn, nkc in xsrcs.items():
            p.dma("sync", xblk[xn][:].rearrange("q (kc t) -> q kc t", t=tb),
                  rearr_fp(io[xn])[:, :, tsl], key="ld",
                  wait=(("td", p.tok("td")),))
        xld = p.tok("ld")
        for sp in specs:
            M = sp["M"]
            epi = sp["epi"]
            for mb in range((M + 127) // 128):
                rows = min(128, M - mb * 128)
                buf = it % 2
                slot = 0
                slots1, slots2 = [], []
                for gi, (wt, xn, K) in enumerate(sp["klist"]):
                    nkc = K // 128
                    p.dma("sync",
                          wtile[buf][0:128, slot * 128:(slot + nkc) * 128]
                          .rearrange("q (kc m) -> q kc m", m=128)[:, :, 0:rows],
                          io[wt].rearrange("(kc q) m -> q kc m", q=128)
                          [:, :, mb * 128: mb * 128 + rows],
                          key="ld", wait=(("td", wtok[buf]),))
                    slots1.append((slot, xn, nkc))
                    slot += nkc
                for gi, (wt, xn, K) in enumerate(sp.get("klist2") or []):
                    nkc = K // 128
                    p.dma("sync",
                          wtile[buf][0:128, slot * 128:(slot + nkc) * 128]
                          .rearrange("q (kc m) -> q kc m", m=128)[:, :, 0:rows],
                          io[wt].rearrange("(kc q) m -> q kc m", q=128)
                          [:, :, mb * 128: mb * 128 + rows],
                          key="ld", wait=(("td", wtok[buf]),))
                    slots2.append((slot, xn, nkc))
                    slot += nkc
                wld = p.tok("ld")
                if epi == "resid":
                    p.dma("sync", aux_sb[buf][0:rows, :],
                          io[sp["aux"]][mb * 128:mb * 128 + rows, tsl],
                          key="ld", wait=(("vd", atok[buf]),))
                auxld = p.tok("ld")

                def mmfn(t, buf=buf, rows=rows, slots1=slots1, slots2=slots2):
                    for slots, pp in ((slots1, ps), (slots2, ps2)):
                        if not slots:
                            continue
                        nktot = sum(nkc for _, _, nkc in slots)
                        ki = 0
                        for slot0, xn, nkc in slots:
                            for kc in range(nkc):
                                for nb in range(tb // 512):
                                    inst = t.matmul(
                                        pp[buf][0:rows, nb * 512:(nb + 1) * 512],
                                        wtile[buf][:, (slot0 + kc) * 128:
                                                   (slot0 + kc) * 128 + rows],
                                        xblk[xn][:, kc * tb + nb * 512:
                                                 kc * tb + (nb + 1) * 512],
                                        start=(ki == 0), stop=(ki == nktot - 1))
                                ki += 1
                    return inst
                p.comp("tensor", mmfn, key="td",
                       wait=(("ld", auxld if epi == "resid" else wld),
                             ("vd", atok[buf]), ("sd", sdtok[buf])))
                wtok[buf] = p.tok("td")
                mmt = wtok[buf]

                out = io[sp["out"]] if sp["out"] else None
                if epi == "bf16" or epi.startswith("scale:"):
                    fac = float(epi.split(":")[1]) if ":" in epi else 1.0

                    def vep(v, buf=buf, rows=rows, fac=fac):
                        if fac != 1.0:
                            return v.tensor_scalar_mul(stageb[buf][0:rows, :],
                                                       ps[buf][0:rows, :], fac)
                        return v.tensor_copy(stageb[buf][0:rows, :],
                                             ps[buf][0:rows, :])
                    p.comp("vector", vep, key="vd",
                           wait=(("td", mmt), ("st", stok[buf])))
                    atok[buf] = p.tok("vd")
                    p.dma("scalar", out[mb * 128:mb * 128 + rows, tsl],
                          stageb[buf][0:rows, :], key="st",
                          wait=(("vd", p.tok("vd")),))
                elif epi == "f32":
                    def vep(v, buf=buf, rows=rows):
                        return v.tensor_copy(stage[buf][0:rows, :],
                                             ps[buf][0:rows, :])
                    p.comp("vector", vep, key="vd",
                           wait=(("td", mmt), ("st", stok[buf])))
                    atok[buf] = p.tok("vd")
                    p.dma("scalar", out[mb * 128:mb * 128 + rows, tsl],
                          stage[buf][0:rows, :], key="st",
                          wait=(("vd", p.tok("vd")),))
                elif epi == "resid":
                    def vep(v, buf=buf, rows=rows):
                        return v.tensor_add(stage[buf][0:rows, :],
                                            ps[buf][0:rows, :],
                                            aux_sb[buf][0:rows, :])
                    p.comp("vector", vep, key="vd",
                           wait=(("td", mmt), ("st", stok[buf])))
                    atok[buf] = p.tok("vd")
                    p.dma("scalar", out[mb * 128:mb * 128 + rows, tsl],
                          stage[buf][0:rows, :], key="st",
                          wait=(("vd", p.tok("vd")),))
                elif epi in ("bitglu", "sqreluglu"):
                    if epi == "bitglu":
                        def sep(s, buf=buf, rows=rows):
                            return s.activation(stage[buf][0:rows, :],
                                                ps[buf][0:rows, :], AF.Sigmoid)
                        p.comp("scalar", sep, key="sd",
                               wait=(("td", mmt), ("vd", atok[buf])))
                        sdtok[buf] = p.tok("sd")

                        def vep(v, buf=buf, rows=rows):
                            return v.tensor_mul(stageb[buf][0:rows, :],
                                                stage[buf][0:rows, :],
                                                ps2[buf][0:rows, :])
                        p.comp("vector", vep, key="vd",
                               wait=(("sd", p.tok("sd")), ("st", stok[buf])))
                    else:
                        def sep(s, buf=buf, rows=rows):
                            return s.activation(stage[buf][0:rows, :],
                                                ps2[buf][0:rows, :], AF.Relu)
                        p.comp("scalar", sep, key="sd",
                               wait=(("td", mmt), ("vd", atok[buf])))
                        sdtok[buf] = p.tok("sd")

                        def vep(v, buf=buf, rows=rows):
                            v.tensor_mul(tmp1[0:rows, :], stage[buf][0:rows, :],
                                         stage[buf][0:rows, :])
                            return v.tensor_mul(stageb[buf][0:rows, :],
                                                tmp1[0:rows, :],
                                                ps[buf][0:rows, :])
                        p.comp("vector", vep, key="vd",
                               wait=(("sd", p.tok("sd")), ("st", stok[buf])))
                    atok[buf] = p.tok("vd")
                    p.dma("scalar", out[mb * 128:mb * 128 + rows, tsl],
                          stageb[buf][0:rows, :], key="st",
                          wait=(("vd", p.tok("vd")),))
                else:
                    raise ValueError(epi)
                stok[buf] = p.tok("st")
                it += 1


def emit_proj0(p, io):
    emit_sweep(p, io, [
        {"klist": [("WQB", "XN0", 2048)], "M": 2048, "out": "QF",
         "epi": "scale:0.08838834764831845"},
        {"klist": [("WKB", "XN0", 2048)], "M": 512, "out": "KF", "epi": "bf16"},
        {"klist": [("WVB", "XN0", 2048)], "M": 512, "out": "VF", "epi": "bf16"},
    ], tb=2048, tag="pj0")


def emit_ffn12(p, io):
    emit_sweep(p, io, [
        {"klist": [("Q1T", "XN0", 2048)], "klist2": [("Q2T", "XN0", 2048)],
         "M": 5632, "out": "HID", "epi": "bitglu"},
    ], tb=1024, tag="f12")


def emit_asm0(p, io):
    emit_sweep(p, io, [
        {"klist": [("Q3T", "HID", 5632), ("WOB", "OA", 2048)],
         "M": 2048, "out": "XF1", "epi": "resid", "aux": "XF0"},
    ], tb=1024, tag="as0")


def emit_proj1(p, io):
    emit_sweep(p, io, [
        {"klist": [("WQKVZT", "XN1", 2048)], "M": 6144, "out": "QKVZ",
         "epi": "bf16"},
        {"klist": [("WBAT", "XN1", 2048)], "M": 32, "out": "BAT",
         "epi": "f32"},
    ], tb=2048, tag="pj1")


def emit_shared(p, io):
    emit_sweep(p, io, [
        {"klist": [("SW1T", "XN1", 2048)], "klist2": [("SWGT", "XN1", 2048)],
         "M": 1024, "out": "HIDS", "epi": "sqreluglu"},
    ], tb=1024, tag="shr")


def emit_asm1(p, io):
    emit_sweep(p, io, [
        {"klist": [("WOUTT", "XATT", 2048), ("SW2T", "HIDS", 1024),
                   ("WUPT", "ACC", 512)],
         "M": 2048, "out": "X2F", "epi": "resid", "aux": "XF1"},
    ], tb=1024, tag="as1")


def emit_xout(p, io):
    nc, es = p.nc, p.es
    x2v = rearr_fp(io["X2F"])
    out = io["OUT"]
    src = [es.enter_context(nc.sbuf_tensor(f"xo_s{i}", [128, 2048], F32))
           for i in range(2)]
    stg = [es.enter_context(nc.sbuf_tensor(f"xo_g{i}", [128, 2048], F32))
           for i in range(2)]
    ps = es.enter_context(nc.psum_tensor("xo_ps", [128, 2048], F32))
    idf = es.enter_context(nc.sbuf_tensor("xo_id", [128, 128], F32))
    p.dma("sync", idf[:], io["IDF32"][:])

    it = 0
    for s in range(B):
        for ti in range(SEQ // 128):
            buf = it % 2
            tcol = s * SEQ + ti * 128
            p.dma("sync", src[buf][:].rearrange("q (fc t) -> q fc t", t=128),
                  x2v[:, :, tcol:tcol + 128], key="ld",
                  wait=(("td", p.tok("td")),))
            ldt = p.tok("ld")
            for g in range(4):
                def tfn(t, buf=buf, g=g):
                    for i in range(4):
                        fc = g * 4 + i
                        inst = t.transpose(
                            ps[:, g * 512 + i * 128: g * 512 + (i + 1) * 128],
                            src[buf][:, fc * 128:(fc + 1) * 128], idf[:])
                    return inst
                p.comp("tensor", tfn, key="td",
                       wait=(("ld", ldt), ("vd", p.tok("vd"))))

                def vfn(v, buf=buf, g=g):
                    return v.tensor_copy(stg[buf][:, g * 512:(g + 1) * 512],
                                         ps[:, g * 512:(g + 1) * 512])
                p.comp("vector", vfn, key="vd",
                       wait=(("td", p.tok("td")), ("st", p.tok("st"))))
            p.dma("scalar", out[s, ti * 128:(ti + 1) * 128, :],
                  stg[buf][:], key="st", wait=(("vd", p.tok("vd")),))
            it += 1


def emit_attn0(p, io):
    nc, es = p.nc, p.es
    NKV, NH = 4, 16
    kfs = es.enter_context(nc.sbuf_tensor("at_kf", [128, NKV * T], BF16))
    vfs = es.enter_context(nc.sbuf_tensor("at_vf", [128, NKV * T], BF16))
    vts = es.enter_context(nc.sbuf_tensor("at_vt", [128, NKV * T], BF16))
    cmsk = es.enter_context(nc.sbuf_tensor("at_cm", [128, 4 * 512], F32))
    idb = es.enter_context(nc.sbuf_tensor("at_id", [128, 128], BF16))
    qhs = [es.enter_context(nc.sbuf_tensor(f"at_q{i}", [128, SEQ], BF16))
           for i in range(2)]
    pbuf = es.enter_context(nc.sbuf_tensor("at_pb", [128, SEQ], BF16))
    pts = es.enter_context(nc.sbuf_tensor("at_pt", [128, 512], BF16))
    ost = [es.enter_context(nc.sbuf_tensor(f"at_o{i}", [128, SEQ], BF16))
           for i in range(2)]
    red = es.enter_context(nc.sbuf_tensor("at_rd", [128, 8], F32))
    psS = es.enter_context(nc.psum_tensor("at_psS", [128, 2048], F32))
    psT = es.enter_context(nc.psum_tensor("at_psT", [128, 512], BF16))
    psO = es.enter_context(nc.psum_tensor("at_psO", [128, 128], F32))

    p.dma("sync", kfs[:].rearrange("q (kv t) -> q kv t", t=T),
          rearr_fp(io["KF"])[:], key="ld")
    p.dma("sync", vfs[:].rearrange("q (kv t) -> q kv t", t=T),
          rearr_fp(io["VF"])[:], key="ld")
    p.dma("sync", cmsk[:].rearrange("q (r k) -> q r k", k=512),
          io["CMASK"].rearrange("r q k -> q r k")[:], key="ld")
    p.dma("gpsimd", idb[:], io["IDBF"][:])
    ld0 = p.tok("ld")

    for kv in range(NKV):
        for s in range(B):
            for grp in range(4):
                def tfn(t, kv=kv, s=s, grp=grp):
                    for i in range(4):
                        ti = grp * 4 + i
                        inst = t.transpose(
                            psT[:, i * 128:(i + 1) * 128],
                            vfs[:, kv * T + s * SEQ + ti * 128:
                                kv * T + s * SEQ + (ti + 1) * 128],
                            idb[:])
                    return inst
                p.comp("tensor", tfn, key="td",
                       wait=(("ld", ld0), ("vd", p.tok("vd"))))

                def vfn(v, kv=kv, s=s, grp=grp):
                    return v.tensor_copy(
                        vts[:, (kv * B + s) * SEQ + grp * 512:
                            (kv * B + s) * SEQ + (grp + 1) * 512],
                        psT[:])
                p.comp("vector", vfn, key="vd", wait=(("td", p.tok("td")),))
    vt_done = p.tok("vd")

    qfv = io["QF"]
    it = 0
    for s in range(B):
        for h in range(NH):
            kv = h // 4
            buf = it % 2
            p.dma("sync", qhs[buf][:],
                  qfv[h * 128:(h + 1) * 128, s * SEQ:(s + 1) * SEQ],
                  key="ld", wait=(("td", p.tok("td")),))
            qld = p.tok("ld")
            for qi in range(SEQ // 128):
                nkc = qi // 4 + 1
                kbase = kv * T + s * SEQ

                def t_qk(t, buf=buf, qi=qi, nkc=nkc, kbase=kbase):
                    for kc in range(nkc):
                        inst = t.matmul(
                            psS[:, kc * 512:(kc + 1) * 512],
                            qhs[buf][:, qi * 128:(qi + 1) * 128],
                            kfs[:, kbase + kc * 512: kbase + (kc + 1) * 512],
                            start=True, stop=True)
                    return inst
                p.comp("tensor", t_qk, key="td",
                       wait=(("ld", qld), ("vd", p.tok("vd")),
                             ("sd", p.tok("sd"))))
                qkt = p.tok("td")

                def v_mask(v, qi=qi, nkc=nkc):
                    dk = (nkc - 1) * 512
                    v.tensor_add(psS[:, dk:dk + 512], psS[:, dk:dk + 512],
                                 cmsk[:, (qi % 4) * 512:(qi % 4 + 1) * 512])
                    return v.reduce_max(red[:, 0:1], psS[:, 0:nkc * 512],
                                        axis=AX.X, negate=True)
                p.comp("vector", v_mask, key="vd", wait=(("td", qkt),))
                mskt = p.tok("vd")

                def s_exp(s_, qi=qi, nkc=nkc):
                    for kc in range(nkc):
                        inst = s_.activation(
                            pbuf[:, kc * 512:(kc + 1) * 512],
                            psS[:, kc * 512:(kc + 1) * 512], AF.Exp,
                            bias=red[:, 0:1],
                            accum_out=red[:, 1 + kc:2 + kc])
                    return inst
                p.comp("scalar", s_exp, key="sd", wait=(("vd", mskt),))
                expt = p.tok("sd")

                def v_lsum(v, nkc=nkc):
                    return v.reduce_sum(red[:, 5:6], red[:, 1:1 + nkc],
                                        axis=AX.X)
                p.comp("vector", v_lsum, key="vd", wait=(("sd", expt),))

                def v_rec(v):
                    return v.reciprocal(red[:, 6:7], red[:, 5:6])
                p.comp("vector", v_rec, key="vd")

                def v_norm(v, nkc=nkc):
                    return v.tensor_scalar_mul(pbuf[:, 0:nkc * 512],
                                               pbuf[:, 0:nkc * 512],
                                               red[:, 6:7])
                p.comp("vector", v_norm, key="vd")
                pnt = p.tok("vd")

                for kc in range(nkc):
                    def t_tr(t, kc=kc):
                        for i in range(4):
                            inst = t.transpose(
                                psT[:, i * 128:(i + 1) * 128],
                                pbuf[:, kc * 512 + i * 128:
                                     kc * 512 + (i + 1) * 128], idb[:])
                        return inst
                    p.comp("tensor", t_tr, key="td",
                           wait=(("vd", pnt if kc == 0 else p.tok("vd")),))

                    def v_pt(v):
                        return v.tensor_copy(pts[:], psT[:])
                    p.comp("vector", v_pt, key="vd",
                           wait=(("td", p.tok("td")),))
                    ptt = p.tok("vd")

                    def t_pv(t, kc=kc, nkc=nkc, kv=kv, s=s):
                        for i in range(4):
                            ki = kc * 4 + i
                            inst = t.matmul(
                                psO[:],
                                vts[:, (kv * B + s) * SEQ + ki * 128:
                                    (kv * B + s) * SEQ + (ki + 1) * 128],
                                pts[:, i * 128:(i + 1) * 128],
                                start=(ki == 0), stop=(ki == nkc * 4 - 1))
                        return inst
                    p.comp("tensor", t_pv, key="td", wait=(("vd", ptt),))
                pvt = p.tok("td")

                def v_o(v, buf=buf, qi=qi):
                    return v.tensor_copy(ost[buf][:, qi * 128:(qi + 1) * 128],
                                         psO[:])
                p.comp("vector", v_o, key="vd",
                       wait=(("td", pvt), ("st", p.tok("st"))))
            p.dma("scalar", io["OA"][h * 128:(h + 1) * 128,
                                     s * SEQ:(s + 1) * SEQ],
                  ost[buf][:], key="st", wait=(("vd", p.tok("vd")),))
            it += 1


def emit_conv(p, io):
    nc, es = p.nc, p.es
    NKH, NVH = 8, 16
    QSCALE = 0.08838834764831845

    b16 = es.enter_context(nc.sbuf_tensor("cv_b16", [16, T], F32))
    a16 = es.enter_context(nc.sbuf_tensor("cv_a16", [16, T], F32))
    beta = es.enter_context(nc.sbuf_tensor("cv_beta", [16, T], F32))
    grow = es.enter_context(nc.sbuf_tensor("cv_g", [16, T], F32))
    crow = es.enter_context(nc.sbuf_tensor("cv_c", [16, T], F32))
    trow = [es.enter_context(nc.sbuf_tensor(f"cv_t{i}", [16, T], F32))
            for i in range(2)]
    zrow = es.enter_context(nc.sbuf_tensor("cv_z", [16, 128], F32))
    alog = es.enter_context(nc.sbuf_tensor("cv_al", [16, 1], F32))
    dtb = es.enter_context(nc.sbuf_tensor("cv_dt", [16, 1], F32))
    nega = es.enter_context(nc.sbuf_tensor("cv_na", [16, 1], F32))
    cws = es.enter_context(nc.sbuf_tensor("cv_cw", [128, 32 * 4], F32))
    ones = es.enter_context(nc.sbuf_tensor("cv_on", [128, 1], F32))
    ones1 = es.enter_context(nc.sbuf_tensor("cv_o1", [1, 128], F32))
    itile = [es.enter_context(nc.sbuf_tensor(f"cv_i{i}", [128, 515], BF16))
             for i in range(2)]
    acc = [es.enter_context(nc.sbuf_tensor(f"cv_ac{i}", [128, 512], F32))
           for i in range(2)]
    co = es.enter_context(nc.sbuf_tensor("cv_co", [128, 512], BF16))
    sq = es.enter_context(nc.sbuf_tensor("cv_sq", [128, 512], F32))
    rs = es.enter_context(nc.sbuf_tensor("cv_rs", [1, 512], F32))
    outt = [es.enter_context(nc.sbuf_tensor(f"cv_ot{i}", [128, 512], BF16))
            for i in range(2)]
    psn = es.enter_context(nc.psum_tensor("cv_pn", [1, 512], F32))
    psb = es.enter_context(nc.psum_tensor("cv_pb", [128, 512], F32))

    for v in range(16):
        g_, r_ = v // 2, v % 2
        p.dma("sync", b16[v:v + 1, :], io["BAT"][4 * g_ + r_:4 * g_ + r_ + 1, :],
              key="ld")
        p.dma("sync", a16[v:v + 1, :],
              io["BAT"][4 * g_ + 2 + r_:4 * g_ + 3 + r_, :], key="ld")
    p.dma("sync", alog[:], io["L1_attn_A_log"][:], key="ld")
    p.dma("sync", dtb[:], io["L1_attn_dt_bias"][:], key="ld")
    p.dma("sync", cws[:].rearrange("q (rc j) -> q rc j", j=4),
          io["L1_attn_conv_w"].rearrange("(rc q) j -> q rc j", q=128)[:],
          key="ld")
    p.dma("sync", ones[:], io["ONES128"][:], key="ld")
    p.dma("sync", ones1[:], io["ONES1"][:], key="ld")
    ld0 = p.tok("ld")

    def s_beta(s):
        s.activation(nega[:], alog[:], AF.Exp)
        return s.activation(beta[:], b16[:], AF.Sigmoid)
    p.comp("scalar", s_beta, key="sd", wait=(("ld", ld0),))

    def v_dt(v):
        return v.tensor_scalar_add(trow[0][:], a16[:], dtb[:, 0:1])
    p.comp("vector", v_dt, key="vd", wait=(("ld", ld0),))

    def s_sp1(s):
        s.activation(trow[1][:], trow[0][:], AF.Abs)
        return s.activation(trow[1][:], trow[1][:], AF.Exp, scale=-1.0)
    p.comp("scalar", s_sp1, key="sd", wait=(("vd", p.tok("vd")),))

    def s_sp2(s):
        return s.activation(trow[1][:], trow[1][:], AF.Ln, bias=1.0)
    p.comp("scalar", s_sp2, key="sd")
    spt = p.tok("sd")

    def v_g(v):
        v.tensor_scalar_mul(nega[:], nega[:], -1.0)
        v.tensor_scalar_max(trow[0][:], trow[0][:], 0.0)
        v.tensor_add(grow[:], trow[0][:], trow[1][:])
        return v.tensor_scalar_mul(grow[:], grow[:], nega[:, 0:1])
    p.comp("vector", v_g, key="vd", wait=(("sd", spt),))

    def v_c(v):
        v.memset(trow[0][:], 0.0)
        for n in range(T // 128):
            inst = v.tensor_tensor_scan(
                crow[:, n * 128:(n + 1) * 128],
                grow[:, n * 128:(n + 1) * 128],
                trow[0][:, n * 128:(n + 1) * 128],
                0.0, ALU.add, ALU.add)
        return inst
    p.comp("vector", v_c, key="vd")
    bad = p.tok("vd")
    p.dma("scalar", io["CBE"][0:16, :], crow[:], key="st", wait=(("vd", bad),))
    p.dma("scalar", io["CBE"][16:32, :], beta[:], key="st", wait=(("vd", bad),))

    qkvz = io["QKVZ"]

    def srcrow(rc):
        if rc < 8:
            return rc * 768
        if rc < 16:
            return (rc - 8) * 768 + 128
        vh = rc - 16
        return (vh // 2) * 768 + 256 + (vh % 2) * 128

    it = 0
    for rc in range(32):
        base = srcrow(rc)
        kind = "q" if rc < 8 else ("k" if rc < 16 else "v")
        for s in range(B):
            for tci in range(SEQ // 512):
                buf = it % 2
                c0 = s * SEQ + tci * 512

                def v_zero(v, buf=buf):
                    return v.memset(itile[buf][:, 0:3], 0.0)
                if tci == 0:
                    p.comp("vector", v_zero, key="vd",
                           wait=(("td", p.tok("td")), ("sd", p.tok("sd"))))
                    p.dma("sync", itile[buf][:, 3:515],
                          qkvz[base:base + 128, c0:c0 + 512], key="ld",
                          wait=(("vd", p.tok("vd")),))
                else:
                    p.dma("sync", itile[buf][:, 0:515],
                          qkvz[base:base + 128, c0 - 3:c0 + 512], key="ld",
                          wait=(("td", p.tok("td")), ("sd", p.tok("sd")),
                                ("vd", p.tok("vd"))))
                ldt = p.tok("ld")

                def v_conv(v, buf=buf, rc=rc):
                    v.tensor_scalar_mul(acc[0][:], itile[buf][:, 0:512],
                                        cws[:, rc * 4:rc * 4 + 1])
                    v.scalar_tensor_tensor(acc[1][:], itile[buf][:, 1:513],
                                           cws[:, rc * 4 + 1:rc * 4 + 2],
                                           acc[0][:], ALU.mult, ALU.add)
                    v.scalar_tensor_tensor(acc[0][:], itile[buf][:, 2:514],
                                           cws[:, rc * 4 + 2:rc * 4 + 3],
                                           acc[1][:], ALU.mult, ALU.add)
                    return v.scalar_tensor_tensor(
                        acc[1][:], itile[buf][:, 3:515],
                        cws[:, rc * 4 + 3:rc * 4 + 4],
                        acc[0][:], ALU.mult, ALU.add)
                p.comp("vector", v_conv, key="vd",
                       wait=(("ld", ldt), ("sd", p.tok("sd"))))
                cvt = p.tok("vd")

                def s_silu(s_):
                    return s_.activation(co[:], acc[1][:], AF.Silu)
                p.comp("scalar", s_silu, key="sd", wait=(("vd", cvt),))
                silt = p.tok("sd")

                if kind == "v":
                    def v_out(v, buf=buf):
                        return v.tensor_copy(outt[buf][:], co[:])
                    p.comp("vector", v_out, key="vd",
                           wait=(("sd", silt), ("st", p.tok("st"))))
                    p.dma("scalar",
                          io["VC"][(rc - 16) * 128:(rc - 15) * 128,
                                   c0:c0 + 512],
                          outt[buf][:], key="st", wait=(("vd", p.tok("vd")),))
                else:
                    def s_sq(s_):
                        return s_.activation(sq[:], co[:], AF.Square)
                    p.comp("scalar", s_sq, key="sd")
                    sqt = p.tok("sd")

                    def t_n(t):
                        return t.matmul(psn[0:1, :], ones[:], sq[:],
                                        start=True, stop=True)
                    p.comp("tensor", t_n, key="td",
                           wait=(("sd", sqt), ("vd", p.tok("vd"))))

                    def v_eps(v):
                        return v.tensor_scalar_add(rs[0:1, :], psn[0:1, :],
                                                   1e-6)
                    p.comp("vector", v_eps, key="vd",
                           wait=(("td", p.tok("td")),))

                    def s_rt(s_):
                        return s_.activation(rs[0:1, :], rs[0:1, :], AF.Sqrt)
                    p.comp("scalar", s_rt, key="sd",
                           wait=(("vd", p.tok("vd")),))

                    def v_rc(v):
                        return v.reciprocal(rs[0:1, :], rs[0:1, :])
                    p.comp("vector", v_rc, key="vd", wait=(("sd", p.tok("sd")),))

                    def t_bc(t):
                        return t.matmul(psb[:], ones1[:], rs[0:1, :],
                                        start=True, stop=True)
                    p.comp("tensor", t_bc, key="td", wait=(("vd", p.tok("vd")),))
                    bct = p.tok("td")

                    if kind == "q":
                        def v_out(v, buf=buf):
                            return v.scalar_tensor_tensor(
                                outt[buf][:], co[:], QSCALE, psb[:],
                                ALU.mult, ALU.mult)
                    else:
                        def v_out(v, buf=buf):
                            return v.tensor_mul(outt[buf][:], co[:], psb[:])
                    p.comp("vector", v_out, key="vd",
                           wait=(("td", bct), ("st", p.tok("st"))))
                    dst = io["QN"] if kind == "q" else io["KN"]
                    g = rc if rc < 8 else rc - 8
                    p.dma("scalar", dst[g * 128:(g + 1) * 128, c0:c0 + 512],
                          outt[buf][:], key="st", wait=(("vd", p.tok("vd")),))
                it += 1

    for vh in range(NVH):
        zbase = (vh // 2) * 768 + 512 + (vh % 2) * 128
        for s in range(B):
            for tci in range(SEQ // 512):
                buf = it % 2
                c0 = s * SEQ + tci * 512
                p.dma("sync", itile[buf][:, 0:512],
                      qkvz[zbase:zbase + 128, c0:c0 + 512], key="ld",
                      wait=(("sd", p.tok("sd")),))
                ldt = p.tok("ld")

                def s_z(s_, buf=buf):
                    return s_.activation(outt[buf][:], itile[buf][:, 0:512],
                                         AF.Silu)
                p.comp("scalar", s_z, key="sd",
                       wait=(("ld", ldt), ("st", p.tok("st"))))
                p.dma("scalar", io["ZS"][vh * 128:(vh + 1) * 128, c0:c0 + 512],
                      outt[buf][:], key="st", wait=(("sd", p.tok("sd")),))
                it += 1


def emit_gatex(p, io):
    emit_sweep(p, io, [
        {"klist": [("WDOWNT", "XN1", 2048)], "M": 512, "out": "XL",
         "epi": "bf16"},
    ], tb=2048, tag="gtx")


def emit_gate(p, io):
    nc, es = p.nc, p.es
    xblk = es.enter_context(nc.sbuf_tensor("gt_x", [128, 16 * 2048], BF16))
    wg = es.enter_context(nc.sbuf_tensor("gt_w", [128, 16 * 16], BF16))
    prb = es.enter_context(nc.sbuf_tensor("gt_pr", [16, 2048], F32))
    ptm = es.enter_context(nc.sbuf_tensor("gt_pt", [128, 16], F32))
    yy = es.enter_context(nc.sbuf_tensor("gt_y", [128, 16], F32))
    rk = [es.enter_context(nc.sbuf_tensor(f"gt_rk{i}", [128, 16], F32))
          for i in range(2)]
    bb = es.enter_context(nc.sbuf_tensor("gt_bb", [128, 16], F32))
    brow = es.enter_context(nc.sbuf_tensor("gt_br", [1, 16], F32))
    den = es.enter_context(nc.sbuf_tensor("gt_dn", [128, 2], F32))
    wds = es.enter_context(nc.sbuf_tensor("gt_wd", [16, 2048], F32))
    idf = es.enter_context(nc.sbuf_tensor("gt_id", [128, 128], F32))
    ones1 = es.enter_context(nc.sbuf_tensor("gt_o1", [1, 128], F32))
    psg = es.enter_context(nc.psum_tensor("gt_ps", [16, 2048], F32))
    pst = es.enter_context(nc.psum_tensor("gt_pt2", [128, 16], F32))
    psb = es.enter_context(nc.psum_tensor("gt_pb", [128, 16], F32))
    psw = es.enter_context(nc.psum_tensor("gt_pw", [16, 128], F32))

    p.dma("sync", idf[:], io["IDF32"][:], key="ld")
    p.dma("sync", ones1[:], io["ONES1"][:], key="ld")
    p.dma("sync", brow[0:1, :], io["L1_ffn_biases"][:], key="ld")
    p.dma("sync", wg[:].rearrange("q (kc m) -> q kc m", m=16),
          io["WGT"].rearrange("(kc q) m -> q kc m", q=128)[:], key="ld")
    ld0 = p.tok("ld")

    def t_bb(t):
        return t.matmul(psb[:], ones1[:], brow[0:1, :], start=True, stop=True)
    p.comp("tensor", t_bb, key="td", wait=(("ld", ld0),))

    def v_bb(v):
        return v.tensor_copy(bb[:], psb[:])
    p.comp("vector", v_bb, key="vd", wait=(("td", p.tok("td")),))

    for tbi in range(T // 2048):
        tsl = slice(tbi * 2048, (tbi + 1) * 2048)
        p.dma("sync", xblk[:].rearrange("q (kc t) -> q kc t", t=2048),
              rearr_fp(io["XN1"])[:, :, tsl], key="ld",
              wait=(("td", p.tok("td")),))
        xld = p.tok("ld")

        def t_pr(t):
            for kc in range(16):
                for nb in range(4):
                    inst = t.matmul(psg[0:16, nb * 512:(nb + 1) * 512],
                                    wg[:, kc * 16:(kc + 1) * 16],
                                    xblk[:, kc * 2048 + nb * 512:
                                         kc * 2048 + (nb + 1) * 512],
                                    start=(kc == 0), stop=(kc == 15))
            return inst
        p.comp("tensor", t_pr, key="td",
               wait=(("ld", xld), ("sd", p.tok("sd"))))

        def s_sig(s_):
            return s_.activation(prb[:], psg[0:16, :], AF.Sigmoid)
        p.comp("scalar", s_sig, key="sd", wait=(("td", p.tok("td")),))
        sigt = p.tok("sd")

        for i in range(16):
            def t_tr(t, i=i):
                return t.transpose(pst[:, 0:16], prb[:, i * 128:(i + 1) * 128],
                                   idf[0:16, 0:16])
            p.comp("tensor", t_tr, key="td",
                   wait=(("sd", sigt), ("vd", p.tok("vd"))))

            p.comp("vector", lambda v: v.tensor_copy(ptm[:], pst[:, 0:16]),
                   key="vd", wait=(("td", p.tok("td")),))
            p.comp("vector", lambda v: v.tensor_add(yy[:], ptm[:], bb[:]),
                   key="vd")
            p.comp("vector", lambda v: v.tensor_single_scalar(
                rk[0][:], yy[:], yy[:, 0:1], ALU.is_lt), key="vd")
            for j in range(1, 16):
                p.comp("vector", lambda v, j=j: v.scalar_tensor_tensor(
                    rk[j % 2][:], yy[:], yy[:, j:j + 1], rk[(j + 1) % 2][:],
                    ALU.is_lt, ALU.add), key="vd")
            p.comp("vector", lambda v: v.tensor_single_scalar(
                rk[0][:], rk[1][:], 7.5, ALU.is_le), key="vd")
            p.comp("vector", lambda v: v.tensor_mul(rk[1][:], ptm[:], rk[0][:]),
                   key="vd")
            p.comp("vector", lambda v: v.reduce_sum(den[:, 0:1], rk[1][:],
                                                    axis=AX.X), key="vd")
            p.comp("vector", lambda v: v.reciprocal(den[:, 1:2], den[:, 0:1]),
                   key="vd")
            p.comp("vector", lambda v: v.tensor_scalar(
                rk[0][:], rk[1][:], den[:, 1:2], 2.5, ALU.mult, ALU.mult),
                key="vd")

            def t_wt(t, i=i):
                return t.transpose(psw[0:16, :], rk[0][:], idf[:])
            p.comp("tensor", t_wt, key="td", wait=(("vd", p.tok("vd")),))

            def v_wd(v, i=i):
                return v.tensor_copy(wds[:, i * 128:(i + 1) * 128],
                                     psw[0:16, :])
            p.comp("vector", v_wd, key="vd",
                   wait=(("td", p.tok("td")), ("st", p.tok("st"))))
        p.dma("scalar", io["WD"][:, tsl], wds[:], key="st",
              wait=(("vd", p.tok("vd")),))


def emit_scan(p, io):
    nc, es = p.nc, p.es
    NKH, NVH, C = 8, 16, 128
    NCH = SEQ // C
    G = 1

    qn = es.enter_context(nc.sbuf_tensor("sc_q", [128, NKH * C], BF16))
    kn = es.enter_context(nc.sbuf_tensor("sc_k", [128, NKH * C], BF16))
    vc = es.enter_context(nc.sbuf_tensor("sc_v", [128, NVH * C], BF16))
    zs = es.enter_context(nc.sbuf_tensor("sc_z", [128, NVH * C], BF16))
    cbe = es.enter_context(nc.sbuf_tensor("sc_cb", [32, C], F32))
    ccol = es.enter_context(nc.sbuf_tensor("sc_cc", [128, 16], F32))
    bcol = es.enter_context(nc.sbuf_tensor("sc_bc", [128, 16], F32))
    gcol = es.enter_context(nc.sbuf_tensor("sc_gc", [128, 16], F32))
    cbcol = es.enter_context(nc.sbuf_tensor("sc_cbc", [128, 16], F32))
    nccol = es.enter_context(nc.sbuf_tensor("sc_nc", [128, 16], F32))
    bgcol = es.enter_context(nc.sbuf_tensor("sc_bg", [128, 16], F32))
    gccol = es.enter_context(nc.sbuf_tensor("sc_gC", [128, 16], F32))
    dccol = es.enter_context(nc.sbuf_tensor("sc_dc", [128, 16], F32))
    ccr = es.enter_context(nc.sbuf_tensor("sc_ccr", [1, 16], F32))
    mpos = es.enter_context(nc.sbuf_tensor("sc_mp", [128, 128], F32))
    minc = es.enter_context(nc.sbuf_tensor("sc_mi", [128, 128], F32))
    eh = es.enter_context(nc.sbuf_tensor("sc_eh", [16, 16 * 128], F32))
    idf = es.enter_context(nc.sbuf_tensor("sc_id", [128, 128], F32))
    idb = es.enter_context(nc.sbuf_tensor("sc_idb", [128, 128], BF16))
    ones1 = es.enter_context(nc.sbuf_tensor("sc_o1", [1, 128], F32))
    nwbc = es.enter_context(nc.sbuf_tensor("sc_nw", [128, 128], F32))
    nwrow = es.enter_context(nc.sbuf_tensor("sc_nwr", [1, 128], F32))
    sfp = es.enter_context(nc.sbuf_tensor("sc_sf", [128, NVH * 128], F32))
    sbf = es.enter_context(nc.sbuf_tensor("sc_sb", [128, NVH * 128], BF16))
    ktm = es.enter_context(nc.sbuf_tensor("sc_kt", [128, NKH * 128], BF16))
    kbt = es.enter_context(nc.sbuf_tensor("sc_kb", [128, G * 128], BF16))
    cbc = es.enter_context(nc.sbuf_tensor("sc_cbc2", [128, G * 128], F32))
    tmsk = es.enter_context(nc.sbuf_tensor("sc_tm", [128, G * 2 * 128], F32))
    dexp = es.enter_context(nc.sbuf_tensor("sc_dx", [128, G * 128], F32))
    aqt = es.enter_context(nc.sbuf_tensor("sc_aq", [128, G * 128], BF16))
    lsb = es.enter_context(nc.sbuf_tensor("sc_l", [128, G * 128], BF16))
    ltsb = es.enter_context(nc.sbuf_tensor("sc_lt", [128, G * 128], BF16))
    pwr = es.enter_context(nc.sbuf_tensor("sc_pw", [128, G * 2 * 128], BF16))
    ptl = es.enter_context(nc.sbuf_tensor("sc_ptl", [128, G * 6 * 128], BF16))
    rsb = es.enter_context(nc.sbuf_tensor("sc_r", [128, G * 128], BF16))
    xsb = es.enter_context(nc.sbuf_tensor("sc_x", [128, G * 2 * 128], BF16))
    t1 = es.enter_context(nc.sbuf_tensor("sc_t1", [128, G * 128], F32))
    otm = es.enter_context(nc.sbuf_tensor("sc_ot", [128, G * 128], F32))
    xfg = es.enter_context(nc.sbuf_tensor("sc_xf", [128, NVH * 128], F32))
    ssq = es.enter_context(nc.sbuf_tensor("sc_sq", [128, NVH], F32))
    sqv = es.enter_context(nc.sbuf_tensor("sc_sqv", [128, G * 128], F32))
    rscol = es.enter_context(nc.sbuf_tensor("sc_rs", [128, 16], F32))
    xnrm = es.enter_context(nc.sbuf_tensor("sc_xn", [128, NVH * 128], BF16))
    stg = es.enter_context(nc.sbuf_tensor("sc_st", [128, NVH * 128], BF16))
    pgram = es.enter_context(nc.psum_tensor("sc_pg", [128, G * 2 * 128], F32))
    pmisc = es.enter_context(nc.psum_tensor("sc_pm", [128, G * 128 + 128], F32))
    psol = es.enter_context(nc.psum_tensor("sc_ps", [128, G * 2 * 128], F32))
    ptr = es.enter_context(nc.psum_tensor("sc_ptr", [128, G * 512], BF16))
    pr32 = es.enter_context(nc.psum_tensor("sc_pr32", [128, G * 256], F32))

    p.dma("sync", mpos[:], io["MASKS_POS"][:], key="ld")
    p.dma("sync", minc[:], io["MASKU_INC"][:], key="ld")
    p.dma("sync", eh[:].rearrange("q (h m) -> q h m", m=128),
          io["EH"].rearrange("h q m -> q h m")[:], key="ld")
    p.dma("sync", idf[:], io["IDF32"][:], key="ld")
    p.dma("gpsimd", idb[:], io["IDBF"][:], key="ld")
    p.dma("sync", ones1[:], io["ONES1"][:], key="ld")
    p.dma("sync", nwrow[0:1, :], io["L1_attn_norm_weight"][:], key="ld")
    ld0 = p.tok("ld")

    def t_nw(t):
        return t.matmul(pmisc[:, 0:128], ones1[:], nwrow[0:1, :],
                        start=True, stop=True)
    p.comp("tensor", t_nw, key="td", wait=(("ld", ld0),))

    def v_nw(v):
        return v.tensor_copy(nwbc[:], pmisc[:, 0:128])
    p.comp("vector", v_nw, key="vd", wait=(("td", p.tok("td")),))

    for s in range(B):
        def v_s0(v):
            v.memset(sfp[:], 0.0)
            return v.memset(sbf[:], 0.0)
        p.comp("vector", v_s0, key="vd", wait=(("td", p.tok("td")),))

        for n in range(NCH):
            csl = slice(s * SEQ + n * C, s * SEQ + (n + 1) * C)
            ldw = (("td", p.tok("td")), ("vd", p.tok("vd")))
            p.dma("sync", qn[:].rearrange("q (g t) -> q g t", t=C),
                  rearr_fp(io["QN"])[:, :, csl], key="ld", wait=ldw)
            p.dma("sync", kn[:].rearrange("q (g t) -> q g t", t=C),
                  rearr_fp(io["KN"])[:, :, csl], key="ld")
            p.dma("sync", vc[:].rearrange("q (h t) -> q h t", t=C),
                  rearr_fp(io["VC"])[:, :, csl], key="ld")
            p.dma("sync", zs[:].rearrange("q (h t) -> q h t", t=C),
                  rearr_fp(io["ZS"])[:, :, csl], key="ld")
            p.dma("sync", cbe[:], io["CBE"][:, csl], key="ld")
            ldt = p.tok("ld")

            def t_cb(t):
                return t.transpose(pmisc[:, 0:32], cbe[:], idf[0:32, 0:32])
            p.comp("tensor", t_cb, key="td",
                   wait=(("ld", ldt), ("vd", p.tok("vd"))))

            def v_cb(v):
                v.tensor_copy(ccol[:], pmisc[:, 0:16])
                return v.tensor_copy(bcol[:], pmisc[:, 16:32])
            p.comp("vector", v_cb, key="vd", wait=(("td", p.tok("td")),))

            def t_ccr(t):
                return t.transpose(pmisc[0:1, 32:48], cbe[0:16, C - 1:C],
                                   idf[0:16, 0:16])
            p.comp("tensor", t_ccr, key="td", wait=(("vd", p.tok("vd")),))

            def v_ccr(v):
                return v.tensor_copy(ccr[0:1, :], pmisc[0:1, 32:48])
            p.comp("vector", v_ccr, key="vd", wait=(("td", p.tok("td")),))

            def t_bcc(t):
                return t.matmul(pmisc[:, 64:80], ones1[:], ccr[0:1, :],
                                start=True, stop=True)
            p.comp("tensor", t_bcc, key="td", wait=(("vd", p.tok("vd")),))
            bcct = p.tok("td")

            def s_prep(s_):
                s_.activation(gcol[:], ccol[:], AF.Exp)
                s_.activation(cbcol[:], bcol[:], AF.Ln)
                return s_.activation(gccol[:], pmisc[:, 64:80], AF.Exp)
            p.comp("scalar", s_prep, key="sd", wait=(("td", bcct),))

            def v_prep(v):
                v.tensor_add(cbcol[:], cbcol[:], ccol[:])
                v.tensor_scalar_mul(nccol[:], ccol[:], -1.0)
                v.tensor_mul(bgcol[:], bcol[:], gcol[:])
                return v.tensor_sub(dccol[:], pmisc[:, 64:80], ccol[:])
            p.comp("vector", v_prep, key="vd", wait=(("sd", p.tok("sd")),))

            def s_dc(s_):
                return s_.activation(dccol[:], dccol[:], AF.Exp)
            p.comp("scalar", s_dc, key="sd", wait=(("vd", p.tok("vd")),))
            prept = p.tok("sd")

            for g0 in range(0, NKH, 4):
                def t_kt(t, g0=g0):
                    for i in range(4):
                        inst = t.transpose(ptr[:, i * 128:(i + 1) * 128],
                                           kn[:, (g0 + i) * C:(g0 + i + 1) * C],
                                           idb[:])
                    return inst
                p.comp("tensor", t_kt, key="td",
                       wait=(("ld", ldt), ("vd", p.tok("vd"))))

                def v_kt(v, g0=g0):
                    return v.tensor_copy(ktm[:, g0 * 128:(g0 + 4) * 128],
                                         ptr[:, 0:512])
                p.comp("vector", v_kt, key="vd", wait=(("td", p.tok("td")),))
            ktt = p.tok("vd")

            for h0 in range(0, NVH, G):
                hs = list(range(h0, h0 + G))

                def t_gram(t, hs=hs):
                    for i, h in enumerate(hs):
                        g = h // 2
                        t.matmul(pgram[:, (2 * i) * 128:(2 * i + 1) * 128],
                                 kn[:, g * C:(g + 1) * C],
                                 kn[:, g * C:(g + 1) * C], start=True, stop=True)
                        inst = t.matmul(
                            pgram[:, (2 * i + 1) * 128:(2 * i + 2) * 128],
                            kn[:, g * C:(g + 1) * C],
                            qn[:, g * C:(g + 1) * C], start=True, stop=True)
                    return inst
                p.comp("tensor", t_gram, key="td",
                       wait=(("sd", prept), ("vd", p.tok("vd"))))

                def t_cbc(t, hs=hs):
                    for i, h in enumerate(hs):
                        inst = t.matmul(
                            pmisc[:, 128 + i * 128:128 + (i + 1) * 128],
                            eh[:, h * 128:(h + 1) * 128],
                            cbe[0:16, :], start=True, stop=True)
                    return inst
                p.comp("tensor", t_cbc, key="td", wait=(("vd", p.tok("vd")),))

                def v_cbc(v, hs=hs):
                    for i in range(len(hs)):
                        v.tensor_copy(cbc[:, i * 128:(i + 1) * 128],
                                      pmisc[:, 128 + i * 128:128 + (i + 1) * 128])
                    for i in range(len(hs)):
                        v.tensor_add(tmsk[:, (2 * i) * 128:(2 * i + 1) * 128],
                                     cbc[:, i * 128:(i + 1) * 128], mpos[:])
                        inst = v.tensor_add(
                            tmsk[:, (2 * i + 1) * 128:(2 * i + 2) * 128],
                            cbc[:, i * 128:(i + 1) * 128], minc[:])
                    return inst
                p.comp("vector", v_cbc, key="vd", wait=(("td", p.tok("td")),))

                def s_exp(s_, hs=hs):
                    for i, h in enumerate(hs):
                        s_.activation(dexp[:, i * 128:(i + 1) * 128],
                                      tmsk[:, (2 * i + 1) * 128:(2 * i + 2) * 128],
                                      AF.Exp, bias=nccol[:, h:h + 1])
                        inst = s_.activation(
                            tmsk[:, (2 * i) * 128:(2 * i + 1) * 128],
                            tmsk[:, (2 * i) * 128:(2 * i + 1) * 128], AF.Exp,
                            scale=-1.0, bias=cbcol[:, h:h + 1])
                    return inst
                p.comp("scalar", s_exp, key="sd", wait=(("vd", p.tok("vd")),))

                def v_lm(v, hs=hs):
                    for i in range(len(hs)):
                        v.tensor_mul(aqt[:, i * 128:(i + 1) * 128],
                                     pgram[:, (2 * i + 1) * 128:(2 * i + 2) * 128],
                                     dexp[:, i * 128:(i + 1) * 128])
                        inst = v.tensor_mul(
                            lsb[:, i * 128:(i + 1) * 128],
                            pgram[:, (2 * i) * 128:(2 * i + 1) * 128],
                            tmsk[:, (2 * i) * 128:(2 * i + 1) * 128])
                    return inst
                p.comp("vector", v_lm, key="vd",
                       wait=(("sd", p.tok("sd")), ("td", p.tok("td"))))
                lmt = p.tok("vd")

                def t_lt(t, hs=hs):
                    for i in range(len(hs)):
                        inst = t.transpose(ptr[:, i * 128:(i + 1) * 128],
                                           lsb[:, i * 128:(i + 1) * 128], idb[:])
                    return inst
                p.comp("tensor", t_lt, key="td", wait=(("vd", lmt),))

                def v_lt(v, hs=hs):
                    return v.tensor_copy(ltsb[:, 0:len(hs) * 128],
                                         ptr[:, 0:len(hs) * 128])
                p.comp("vector", v_lt, key="vd", wait=(("td", p.tok("td")),))

                def t_r(t, hs=hs):
                    for i, h in enumerate(hs):
                        g = h // 2
                        t.matmul(pr32[:, (2 * i) * 128:(2 * i + 1) * 128],
                                 kn[:, g * C:(g + 1) * C],
                                 sbf[:, h * 128:(h + 1) * 128],
                                 start=True, stop=True)
                        inst = t.transpose(
                            ptr[:, (G + i) * 128:(G + i + 1) * 128],
                            vc[:, h * C:(h + 1) * C], idb[:])
                    return inst
                p.comp("tensor", t_r, key="td", wait=(("vd", p.tok("vd")),))
                rt = p.tok("td")

                def v_r(v, hs=hs):
                    for i, h in enumerate(hs):
                        v.tensor_scalar_mul(t1[:, i * 128:(i + 1) * 128],
                                            pr32[:, (2 * i) * 128:(2 * i + 1) * 128],
                                            bgcol[:, h:h + 1])
                        inst = v.scalar_tensor_tensor(
                            rsb[:, i * 128:(i + 1) * 128],
                            ptr[:, (G + i) * 128:(G + i + 1) * 128],
                            bcol[:, h:h + 1], t1[:, i * 128:(i + 1) * 128],
                            ALU.mult, ALU.subtract)
                    return inst
                p.comp("vector", v_r, key="vd", wait=(("td", rt),))
                rvt = p.tok("vd")

                def v_p0(v, hs=hs):
                    v.tensor_copy(pwr[:, 0:len(hs) * 128], lsb[:, 0:len(hs) * 128])
                    return v.tensor_copy(ptl[:, 0:len(hs) * 128],
                                         ltsb[:, 0:len(hs) * 128])
                p.comp("vector", v_p0, key="vd")

                def pslot(i, lvl):
                    return slice((i * 6 + lvl) * 128, (i * 6 + lvl + 1) * 128)

                for lvl in range(1, 6):
                    def t_sq(t, hs=hs, lvl=lvl):
                        for i in range(len(hs)):
                            t.matmul(psol[:, (2 * i) * 128:(2 * i + 1) * 128],
                                     pwr[:, i * 128:(i + 1) * 128],
                                     ptl[:, pslot(i, lvl - 1)],
                                     start=True, stop=True)
                            inst = t.matmul(
                                psol[:, (2 * i + 1) * 128:(2 * i + 2) * 128],
                                ptl[:, pslot(i, lvl - 1)],
                                pwr[:, i * 128:(i + 1) * 128],
                                start=True, stop=True)
                        return inst
                    p.comp("tensor", t_sq, key="td", wait=(("vd", p.tok("vd")),))

                    def v_sq(v, hs=hs, lvl=lvl):
                        for i in range(len(hs)):
                            v.tensor_copy(ptl[:, pslot(i, lvl)],
                                          psol[:, (2 * i) * 128:(2 * i + 1) * 128])
                            inst = v.tensor_copy(
                                pwr[:, i * 128:(i + 1) * 128],
                                psol[:, (2 * i + 1) * 128:(2 * i + 2) * 128])
                        return inst
                    p.comp("vector", v_sq, key="vd", wait=(("td", p.tok("td")),))

                def t_sq6(t, hs=hs):
                    for i in range(len(hs)):
                        inst = t.matmul(psol[:, (2 * i) * 128:(2 * i + 1) * 128],
                                        pwr[:, i * 128:(i + 1) * 128],
                                        ptl[:, pslot(i, 5)], start=True, stop=True)
                    return inst
                p.comp("tensor", t_sq6, key="td", wait=(("vd", p.tok("vd")),))

                def v_sq6(v, hs=hs):
                    for i in range(len(hs)):
                        inst = v.tensor_copy(pwr[:, (G + i) * 128:(G + i + 1) * 128],
                                             psol[:, (2 * i) * 128:(2 * i + 1) * 128])
                    return inst
                p.comp("vector", v_sq6, key="vd", wait=(("td", p.tok("td")),))

                def v_x0(v, hs=hs):
                    return v.tensor_copy(xsb[:, 0:len(hs) * 128],
                                         rsb[:, 0:len(hs) * 128])
                p.comp("vector", v_x0, key="vd",
                       wait=(("vd", rvt), ("td", p.tok("td"))))
                xcur = 0
                for ai in range(7):
                    if ai < 6:
                        lvl = 6 - ai
                        def t_ap(t, hs=hs, lvl=lvl, xcur=xcur):
                            for i in range(len(hs)):
                                src_ = (pwr[:, (G + i) * 128:(G + i + 1) * 128]
                                        if lvl == 6 else ptl[:, pslot(i, lvl)])
                                inst = t.matmul(
                                    psol[:, (2 * i + 1) * 128:(2 * i + 2) * 128],
                                    src_,
                                    xsb[:, (xcur * G + i) * 128:(xcur * G + i + 1) * 128],
                                    start=True, stop=True)
                            return inst
                        sign = 1.0
                    else:
                        def t_ap(t, hs=hs, xcur=xcur):
                            for i in range(len(hs)):
                                inst = t.matmul(
                                    psol[:, (2 * i + 1) * 128:(2 * i + 2) * 128],
                                    ltsb[:, i * 128:(i + 1) * 128],
                                    xsb[:, (xcur * G + i) * 128:(xcur * G + i + 1) * 128],
                                    start=True, stop=True)
                            return inst
                        sign = -1.0
                    p.comp("tensor", t_ap, key="td", wait=(("vd", p.tok("vd")),))

                    def v_ap(v, hs=hs, xcur=xcur, sign=sign):
                        for i in range(len(hs)):
                            inst = v.scalar_tensor_tensor(
                                xsb[:, ((1 - xcur) * G + i) * 128:
                                    ((1 - xcur) * G + i + 1) * 128],
                                psol[:, (2 * i + 1) * 128:(2 * i + 2) * 128],
                                sign,
                                xsb[:, (xcur * G + i) * 128:(xcur * G + i + 1) * 128],
                                ALU.mult, ALU.add)
                        return inst
                    p.comp("vector", v_ap, key="vd", wait=(("td", p.tok("td")),))
                    xcur = 1 - xcur

                def t_o(t, hs=hs, xcur=xcur):
                    for i, h in enumerate(hs):
                        g = h // 2
                        t.matmul(pr32[:, (2 * i + 1) * 128:(2 * i + 2) * 128],
                                 qn[:, g * C:(g + 1) * C],
                                 sbf[:, h * 128:(h + 1) * 128],
                                 start=True, stop=True)
                        inst = t.matmul(
                            psol[:, (2 * i) * 128:(2 * i + 1) * 128],
                            aqt[:, i * 128:(i + 1) * 128],
                            xsb[:, (xcur * G + i) * 128:(xcur * G + i + 1) * 128],
                            start=True, stop=True)
                    return inst
                p.comp("tensor", t_o, key="td", wait=(("vd", p.tok("vd")),))
                ot = p.tok("td")

                def v_o(v, hs=hs):
                    for i, h in enumerate(hs):
                        v.tensor_scalar_mul(t1[:, i * 128:(i + 1) * 128],
                                            pr32[:, (2 * i + 1) * 128:(2 * i + 2) * 128],
                                            gcol[:, h:h + 1])
                        inst = v.tensor_add(otm[:, i * 128:(i + 1) * 128],
                                            psol[:, (2 * i) * 128:(2 * i + 1) * 128],
                                            t1[:, i * 128:(i + 1) * 128])
                    return inst
                p.comp("vector", v_o, key="vd", wait=(("td", ot),))

                def t_z(t, hs=hs):
                    for i, h in enumerate(hs):
                        inst = t.transpose(ptr[:, i * 128:(i + 1) * 128],
                                           zs[:, h * C:(h + 1) * C], idb[:])
                    return inst
                p.comp("tensor", t_z, key="td", wait=(("vd", p.tok("vd")),))

                def v_gate(v, hs=hs):
                    for i, h in enumerate(hs):
                        inst = v.tensor_mul(xfg[:, h * 128:(h + 1) * 128],
                                            otm[:, i * 128:(i + 1) * 128],
                                            ptr[:, i * 128:(i + 1) * 128])
                    return inst
                p.comp("vector", v_gate, key="vd", wait=(("td", p.tok("td")),))
                gatet = p.tok("vd")

                def s_sq(s_, hs=hs):
                    for i, h in enumerate(hs):
                        inst = s_.activation(sqv[:, i * 128:(i + 1) * 128],
                                             xfg[:, h * 128:(h + 1) * 128],
                                             AF.Square)
                    return inst
                p.comp("scalar", s_sq, key="sd", wait=(("vd", gatet),))

                def v_var(v, hs=hs):
                    for i, h in enumerate(hs):
                        inst = v.reduce_sum(ssq[:, h:h + 1],
                                            sqv[:, i * 128:(i + 1) * 128],
                                            axis=AX.X)
                    return inst
                p.comp("vector", v_var, key="vd", wait=(("sd", p.tok("sd")),))

                def v_kb(v, hs=hs):
                    for i, h in enumerate(hs):
                        g = h // 2
                        inst = v.tensor_scalar_mul(
                            kbt[:, i * 128:(i + 1) * 128],
                            ktm[:, g * 128:(g + 1) * 128], dccol[:, h:h + 1])
                    return inst
                p.comp("vector", v_kb, key="vd",
                       wait=(("vd", ktt), ("td", p.tok("td"))))
                kbv = p.tok("vd")

                def t_s(t, hs=hs, xcur=xcur):
                    for i in range(len(hs)):
                        inst = t.matmul(
                            psol[:, (2 * i + 1) * 128:(2 * i + 2) * 128],
                            kbt[:, i * 128:(i + 1) * 128],
                            xsb[:, (xcur * G + i) * 128:(xcur * G + i + 1) * 128],
                            start=True, stop=True)
                    return inst
                p.comp("tensor", t_s, key="td", wait=(("vd", kbv),))
                st_ = p.tok("td")

                def v_s(v, hs=hs):
                    for i, h in enumerate(hs):
                        v.scalar_tensor_tensor(
                            sfp[:, h * 128:(h + 1) * 128],
                            sfp[:, h * 128:(h + 1) * 128], gccol[:, h:h + 1],
                            psol[:, (2 * i + 1) * 128:(2 * i + 2) * 128],
                            ALU.mult, ALU.add)
                        inst = v.tensor_copy(sbf[:, h * 128:(h + 1) * 128],
                                             sfp[:, h * 128:(h + 1) * 128])
                    return inst
                p.comp("vector", v_s, key="vd", wait=(("td", st_),))

            def v_eps(v):
                return v.tensor_scalar(rscol[:], ssq[:], 1.0 / 128.0, 1e-6,
                                       ALU.mult, ALU.add)
            p.comp("vector", v_eps, key="vd", wait=(("sd", p.tok("sd")),))

            def s_rstd(s_):
                return s_.activation(rscol[:], rscol[:], AF.Sqrt)
            p.comp("scalar", s_rstd, key="sd", wait=(("vd", p.tok("vd")),))

            def v_rstd(v):
                return v.reciprocal(rscol[:], rscol[:])
            p.comp("vector", v_rstd, key="vd", wait=(("sd", p.tok("sd")),))
            rst = p.tok("vd")

            def v_xn(v):
                for h in range(NVH):
                    inst = v.scalar_tensor_tensor(
                        xnrm[:, h * 128:(h + 1) * 128],
                        xfg[:, h * 128:(h + 1) * 128],
                        rscol[:, h:h + 1], nwbc[:], ALU.mult, ALU.mult)
                return inst
            p.comp("vector", v_xn, key="vd", wait=(("vd", rst),))
            xnt = p.tok("vd")

            for grp in range(4):
                def t_tr(t, grp=grp):
                    for i in range(4):
                        h = grp * 4 + i
                        inst = t.transpose(ptr[:, i * 128:(i + 1) * 128],
                                           xnrm[:, h * 128:(h + 1) * 128],
                                           idb[:])
                    return inst
                p.comp("tensor", t_tr, key="td",
                       wait=(("vd", xnt if grp == 0 else p.tok("vd")),))

                def v_tr(v, grp=grp):
                    return v.tensor_copy(stg[:, grp * 512:(grp + 1) * 512],
                                         ptr[:, 0:512])
                p.comp("vector", v_tr, key="vd",
                       wait=(("td", p.tok("td")), ("st", p.tok("st"))))
            p.dma("scalar", rearr_fp(io["XATT"])[:, :, csl],
                  stg[:].rearrange("q (h t) -> q h t", t=128), key="st",
                  wait=(("vd", p.tok("vd")),))


def emit_moe(p, io):
    nc, es = p.nc, p.es
    E = 16
    TBS = 1024
    xls = es.enter_context(nc.sbuf_tensor("mo_xl", [128, 4 * T], BF16))
    wd = es.enter_context(nc.sbuf_tensor("mo_wd", [16, T], F32))
    accs = es.enter_context(nc.sbuf_tensor("mo_ac", [128, 4 * T], F32))
    w1t = [es.enter_context(nc.sbuf_tensor(f"mo_w1{i}", [128, 4 * 1024], BF16))
           for i in range(2)]
    wgt = [es.enter_context(nc.sbuf_tensor(f"mo_wg{i}", [128, 4 * 1024], BF16))
           for i in range(2)]
    w2t = [es.enter_context(nc.sbuf_tensor(f"mo_w2{i}", [128, 8 * 512], BF16))
           for i in range(2)]
    wbc = es.enter_context(nc.sbuf_tensor("mo_wb", [128, TBS], F32))
    rel = es.enter_context(nc.sbuf_tensor("mo_rl", [128, TBS], F32))
    h1w = es.enter_context(nc.sbuf_tensor("mo_h1", [128, TBS], F32))
    hesb = es.enter_context(nc.sbuf_tensor("mo_he", [128, 8 * TBS], BF16))
    eh = es.enter_context(nc.sbuf_tensor("mo_eh", [16, 16 * 128], F32))
    psg = es.enter_context(nc.psum_tensor("mo_pg", [128, TBS], F32))
    psh = es.enter_context(nc.psum_tensor("mo_ph", [128, TBS], F32))
    psa = es.enter_context(nc.psum_tensor("mo_pa", [128, TBS], F32))
    psw = es.enter_context(nc.psum_tensor("mo_pw", [128, 512], F32))

    p.dma("sync", xls[:].rearrange("q (kc t) -> q kc t", t=T),
          rearr_fp(io["XL"])[:], key="ld")
    p.dma("sync", wd[:], io["WD"][:], key="ld")
    p.dma("sync", eh[:].rearrange("q (h m) -> q h m", m=128),
          io["EH"].rearrange("h q m -> q h m")[:], key="ld")
    ld0 = p.tok("ld")

    wtokm = [0, 0]
    for e in range(E):
        buf = e % 2
        p.dma("sync", w1t[buf][:].rearrange("q (kc m) -> q kc m", m=1024),
              io["RW1T"][e].rearrange("(kc q) m -> q kc m", q=128)[:],
              key="ld", wait=(("td", wtokm[buf]),))
        p.dma("sync", wgt[buf][:].rearrange("q (kc m) -> q kc m", m=1024),
              io["RWGT"][e].rearrange("(kc q) m -> q kc m", q=128)[:],
              key="ld")
        p.dma("sync", w2t[buf][:].rearrange("q (kc m) -> q kc m", m=512),
              io["RW2T"][e].rearrange("(kc q) m -> q kc m", q=128)[:],
              key="ld")
        eld = p.tok("ld")
        for tbi in range(T // TBS):
            t0 = tbi * TBS
            for nb in range(TBS // 512):
                def t_wb(t, e=e, t0=t0, nb=nb):
                    return t.matmul(psw[:], eh[:, e * 128:(e + 1) * 128],
                                    wd[:, t0 + nb * 512:t0 + (nb + 1) * 512],
                                    start=True, stop=True)
                p.comp("tensor", t_wb, key="td",
                       wait=(("ld", eld), ("vd", p.tok("vd"))))

                def v_wb(v, nb=nb):
                    return v.tensor_copy(wbc[:, nb * 512:(nb + 1) * 512],
                                         psw[:])
                p.comp("vector", v_wb, key="vd", wait=(("td", p.tok("td")),))
            wbt = p.tok("vd")

            for m in range(8):
                def t_h(t, buf=buf, m=m, t0=t0):
                    for kc in range(4):
                        for nb in range(TBS // 512):
                            t.matmul(psg[:, nb * 512:(nb + 1) * 512],
                                     wgt[buf][:, kc * 1024 + m * 128:
                                              kc * 1024 + (m + 1) * 128],
                                     xls[:, kc * T + t0 + nb * 512:
                                          kc * T + t0 + (nb + 1) * 512],
                                     start=(kc == 0), stop=(kc == 3))
                    for kc in range(4):
                        for nb in range(TBS // 512):
                            inst = t.matmul(psh[:, nb * 512:(nb + 1) * 512],
                                            w1t[buf][:, kc * 1024 + m * 128:
                                                     kc * 1024 + (m + 1) * 128],
                                            xls[:, kc * T + t0 + nb * 512:
                                                 kc * T + t0 + (nb + 1) * 512],
                                            start=(kc == 0), stop=(kc == 3))
                    return inst
                p.comp("tensor", t_h, key="td",
                       wait=(("vd", p.tok("vd")), ("sd", p.tok("sd"))))
                mmt = p.tok("td")

                def s_rel(s_):
                    return s_.activation(rel[:], psg[:], AF.Relu)
                p.comp("scalar", s_rel, key="sd", wait=(("td", mmt),))

                def v_he(v, m=m):
                    v.tensor_mul(h1w[:], psh[:], wbc[:])
                    v.tensor_mul(rel[:], rel[:], rel[:])
                    return v.tensor_mul(hesb[:, m * TBS:(m + 1) * TBS],
                                        rel[:], h1w[:])
                p.comp("vector", v_he, key="vd", wait=(("sd", p.tok("sd")),))
            het = p.tok("vd")

            for m2 in range(4):
                def t_a(t, buf=buf, m2=m2):
                    for kc in range(8):
                        for nb in range(TBS // 512):
                            inst = t.matmul(psa[:, nb * 512:(nb + 1) * 512],
                                            w2t[buf][:, kc * 512 + m2 * 128:
                                                     kc * 512 + (m2 + 1) * 128],
                                            hesb[:, kc * TBS + nb * 512:
                                                 kc * TBS + (nb + 1) * 512],
                                            start=(kc == 0), stop=(kc == 7))
                    return inst
                p.comp("tensor", t_a, key="td",
                       wait=(("vd", p.tok("vd")), ("sd", p.tok("sd"))))
                at = p.tok("td")

                def v_a(v, e=e, m2=m2, t0=t0):
                    dst = accs[:, m2 * T + t0: m2 * T + t0 + TBS]
                    if e == 0:
                        return v.tensor_copy(dst, psa[:])
                    return v.tensor_add(dst, dst, psa[:])
                p.comp("vector", v_a, key="vd", wait=(("td", at),))
        wtokm[buf] = p.tok("td")

    accv = rearr_fp(io["ACC"])
    for kc in range(4):
        def v_cast(v, kc=kc):
            return v.tensor_copy(hesb[:, 0:T], accs[:, kc * T:(kc + 1) * T])
        p.comp("vector", v_cast, key="vd", wait=(("st", p.tok("st")),))
        p.dma("scalar", accv[:, kc, :], hesb[:, 0:T], key="st",
              wait=(("vd", p.tok("vd")),))


MODULE_INPUTS = {
    "L0_attn_Wq": ([2048, 2048], "f32"), "L0_attn_Wk": ([2048, 512], "f32"),
    "L0_attn_Wv": ([2048, 512], "f32"), "L0_attn_Wo": ([2048, 2048], "f32"),
    "L0_ffn_w1": ([5632, 2048], "f32"), "L0_ffn_w2": ([5632, 2048], "f32"),
    "L0_ffn_w3": ([2048, 5632], "f32"),
    "L1_attn_w_qkvz": ([6144, 2048], "f32"), "L1_attn_w_ba": ([32, 2048], "f32"),
    "L1_attn_conv_w": ([4096, 4], "f32"), "L1_attn_dt_bias": ([16, 1], "f32"),
    "L1_attn_A_log": ([16, 1], "f32"), "L1_attn_norm_weight": ([1, 128], "f32"),
    "L1_attn_w_out": ([2048, 2048], "f32"),
    "L1_ffn_sw1": ([1024, 2048], "f32"), "L1_ffn_swg": ([1024, 2048], "f32"),
    "L1_ffn_sw2": ([2048, 1024], "f32"),
    "L1_ffn_rw1": ([16, 1024, 512], "f32"), "L1_ffn_rwg": ([16, 1024, 512], "f32"),
    "L1_ffn_rw2": ([16, 512, 1024], "f32"),
    "L1_ffn_Wg": ([16, 2048], "f32"), "L1_ffn_Wdown": ([512, 2048], "f32"),
    "L1_ffn_Wup": ([2048, 512], "f32"), "L1_ffn_biases": ([1, 16], "f32"),
    "X0": ([B, SEQ, D], "f32"),
}

SCRATCH = {
    "WQB": ([2048, 2048], "bf16"), "WKB": ([2048, 512], "bf16"),
    "WVB": ([2048, 512], "bf16"), "WOB": ([2048, 2048], "bf16"),
    "WQKVZT": ([2048, 6144], "bf16"), "WBAT": ([2048, 32], "bf16"),
    "WOUTT": ([2048, 2048], "bf16"), "SW1T": ([2048, 1024], "bf16"),
    "SWGT": ([2048, 1024], "bf16"), "SW2T": ([1024, 2048], "bf16"),
    "WDOWNT": ([2048, 512], "bf16"), "WGT": ([2048, 16], "bf16"),
    "WUPT": ([512, 2048], "bf16"),
    "RW1T": ([16, 512, 1024], "bf16"), "RWGT": ([16, 512, 1024], "bf16"),
    "RW2T": ([16, 1024, 512], "bf16"),
    "Q1T": ([2048, 5632], "bf16"), "Q2T": ([2048, 5632], "bf16"),
    "Q3T": ([5632, 2048], "bf16"),
    "XF0": ([2048, T], "f32"), "XN0": ([2048, T], "bf16"),
    "QF": ([2048, T], "bf16"), "KF": ([512, T], "bf16"), "VF": ([512, T], "bf16"),
    "OA": ([2048, T], "bf16"), "HID": ([5632, T], "bf16"),
    "XF1": ([2048, T], "f32"), "XN1": ([2048, T], "bf16"),
    "QKVZ": ([6144, T], "bf16"), "BAT": ([32, T], "f32"),
    "QN": ([1024, T], "bf16"), "KN": ([1024, T], "bf16"),
    "VC": ([2048, T], "bf16"), "ZS": ([2048, T], "bf16"),
    "CBE": ([32, T], "f32"), "XATT": ([2048, T], "bf16"),
    "HIDS": ([1024, T], "bf16"), "XL": ([512, T], "bf16"),
    "WD": ([16, T], "f32"), "ACC": ([512, T], "bf16"),
    "X2F": ([2048, T], "f32"),
    "OUT": ([B, SEQ, D], "f32"),
}


def spec_of(name):
    for tbl in (MODULE_INPUTS, CONST_SPECS, SCRATCH):
        if name in tbl:
            return tbl[name]
    raise KeyError(name)


PHASES = [
    ("prep_w", emit_prep_w,
     ["L0_attn_Wq", "L0_attn_Wk", "L0_attn_Wv", "L0_attn_Wo",
      "L1_attn_w_qkvz", "L1_attn_w_ba", "L1_attn_w_out",
      "L1_ffn_sw1", "L1_ffn_swg", "L1_ffn_sw2", "L1_ffn_Wdown", "L1_ffn_Wg",
      "L1_ffn_Wup", "L1_ffn_rw1", "L1_ffn_rwg", "L1_ffn_rw2", "IDBF"],
     ["WQB", "WKB", "WVB", "WOB", "WQKVZT", "WBAT", "WOUTT", "SW1T", "SWGT",
      "SW2T", "WDOWNT", "WGT", "WUPT", "RW1T", "RWGT", "RW2T"]),
    ("bitq", emit_bitq,
     ["L0_ffn_w1", "L0_ffn_w2", "L0_ffn_w3", "IDBF", "ONES128", "ONES1"],
     ["Q1T", "Q2T", "Q3T"]),
    ("xin", emit_xin, ["X0", "IDF32"], ["XF0"]),
    ("ln0", lambda p, io: emit_ln(p, io, "XF0", "XN0", tag="ln0"),
     ["XF0", "ONES128", "ONES1"], ["XN0"]),
    ("proj0", emit_proj0, ["XN0", "WQB", "WKB", "WVB"], ["QF", "KF", "VF"]),
    ("attn0", emit_attn0, ["QF", "KF", "VF", "CMASK", "IDBF"], ["OA"]),
    ("ffn12", emit_ffn12, ["XN0", "Q1T", "Q2T"], ["HID"]),
    ("asm0", emit_asm0, ["HID", "OA", "Q3T", "WOB", "XF0"], ["XF1"]),
    ("ln1", lambda p, io: emit_ln(p, io, "XF1", "XN1", tag="ln1"),
     ["XF1", "ONES128", "ONES1"], ["XN1"]),
    ("proj1", emit_proj1, ["XN1", "WQKVZT", "WBAT"], ["QKVZ", "BAT"]),
    ("conv", emit_conv,
     ["QKVZ", "BAT", "L1_attn_conv_w", "L1_attn_A_log", "L1_attn_dt_bias",
      "ONES128", "ONES1"],
     ["QN", "KN", "VC", "ZS", "CBE"]),
    ("scan", emit_scan,
     ["QN", "KN", "VC", "ZS", "CBE", "L1_attn_norm_weight", "MASKS_POS",
      "MASKU_INC", "EH", "IDF32", "IDBF", "ONES1"],
     ["XATT"]),
    ("shared", emit_shared, ["XN1", "SW1T", "SWGT"], ["HIDS"]),
    ("gatex", emit_gatex, ["XN1", "WDOWNT"], ["XL"]),
    ("gate", emit_gate, ["XN1", "WGT", "L1_ffn_biases", "IDF32", "ONES1"],
     ["WD"]),
    ("moe", emit_moe, ["XL", "WD", "RW1T", "RWGT", "RW2T", "EH"], ["ACC"]),
    ("asm1", emit_asm1,
     ["XATT", "WOUTT", "HIDS", "SW2T", "ACC", "WUPT", "XF1"], ["X2F"]),
    ("xout", lambda p, io: emit_xout(p, io), ["X2F", "IDF32"], ["OUT"]),
]


def build_phase(name, emit, ins, outs):
    nc = bass.Bass("TRN2", target_bir_lowering=False)
    es = ExitStack()
    with es:
        aps = {}
        for n in ins:
            shape, dt = spec_of(n)
            aps[n] = nc.dram_tensor(n, shape, DT[dt], kind="ExternalInput").ap()
        for n in outs:
            shape, dt = spec_of(n)
            aps[n] = nc.dram_tensor(n, shape, DT[dt], kind="ExternalOutput").ap()
        p = Prog(nc, es, name)
        emit(p, aps)
        p.finalize()
        block = es.enter_context(nc.Block(name))
        p.emit(block)
    return nc


_PHASE_CACHE = {}


def run_phase(name, env):
    for pname, emit, ins, outs in PHASES:
        if pname == name:
            break
    else:
        raise KeyError(name)
    if name not in _PHASE_CACHE:
        _PHASE_CACHE[name] = build_phase(name, emit, ins, outs)
    nc = _PHASE_CACHE[name]
    in_maps = [{n: env[c][n] for n in ins} for c in range(len(env))]
    res = run_bass_kernel_spmd(nc, in_maps, core_ids=list(range(len(env))))
    for c in range(len(env)):
        for n in outs:
            env[c][n] = np.asarray(res.results[c][n])
    return env


def kernel(L0_attn_x, **weights):
    inputs = dict(weights)
    inputs["L0_attn_x"] = L0_attn_x
    consts = host_consts()
    ncores = 8
    env = []
    shaped = {}
    for k in MODULE_INPUTS:
        if k == "X0":
            continue
        shape, dt = MODULE_INPUTS[k]
        shaped[k] = np.ascontiguousarray(
            np.asarray(inputs[k], NPDT[dt]).reshape(shape))
    for c in range(ncores):
        e = dict(shaped)
        e.update(consts)
        e["X0"] = np.ascontiguousarray(
            np.asarray(L0_attn_x, np.float32)[B * c: B * (c + 1)])
        env.append(e)
    for pname, _, _, _ in PHASES:
        env = run_phase(pname, env)
    out = np.concatenate([env[c]["OUT"] for c in range(ncores)], axis=0)
    return np.ascontiguousarray(out.astype(np.float32))


_FUSED_CACHE = []


def build_fused():
    nc = bass.Bass("TRN2", target_bir_lowering=False)
    outer = ExitStack()
    with outer:
        aps = {}
        for n, (shape, dt) in {**MODULE_INPUTS, **CONST_SPECS}.items():
            aps[n] = nc.dram_tensor(n, shape, DT[dt], kind="ExternalInput").ap()
        for n, (shape, dt) in SCRATCH.items():
            kind = "ExternalOutput" if n == "OUT" else "Internal"
            aps[n] = nc.dram_tensor(n, shape, DT[dt], kind=kind).ap()
        p = Prog(nc, outer, "mk", sem_es=outer)
        for name, emit, ins, outs in PHASES:
            with ExitStack() as phase_es:
                p.es = phase_es
                emit(p, aps)
            p.barrier()
        p.finalize()
        block = outer.enter_context(nc.Block("mk"))
        p.emit(block)
    return nc


def kernel_fused(L0_attn_x, **weights):
    inputs = dict(weights)
    consts = host_consts()
    ncores = 8
    shaped = {}
    for k in MODULE_INPUTS:
        if k == "X0":
            continue
        shape, dt = MODULE_INPUTS[k]
        shaped[k] = np.ascontiguousarray(
            np.asarray(inputs[k], NPDT[dt]).reshape(shape))
    shaped.update(consts)
    if not _FUSED_CACHE:
        _FUSED_CACHE.append(build_fused())
    nc = _FUSED_CACHE[0]
    x = np.asarray(L0_attn_x, np.float32)
    in_maps = []
    for c in range(ncores):
        e = dict(shaped)
        e["X0"] = np.ascontiguousarray(x[B * c: B * (c + 1)])
        in_maps.append(e)
    res = run_bass_kernel_spmd(nc, in_maps, core_ids=list(range(ncores)))
    out = np.concatenate([np.asarray(res.results[c]["OUT"])
                          for c in range(ncores)], axis=0)
    return np.ascontiguousarray(out.astype(np.float32))


_ORIG_KERNEL_MULTI = kernel


def kernel(L0_attn_x, **weights):
    return kernel_fused(L0_attn_x, **weights)
